# version 08 (final)
# speedup vs baseline: 1.0696x; 1.0311x over previous
import numpy as np
import concourse.bass as bass
import concourse.mybir as mybir
from concourse.bass_utils import run_bass_kernel_spmd
from concourse.bass_core_rust import IndirectOffsetOnAxis

F32 = mybir.dt.float32
BF16 = mybir.dt.bfloat16
FP16 = mybir.dt.float16
U32 = mybir.dt.uint32
AF = mybir.ActivationFunctionType
ALU = mybir.AluOpType
AX = mybir.AxisListType

S = 16384
DM = 1024
NCORE = 8
TOK = S // NCORE
DH = 64
NHASH = 4
NBUCK = 512
CHUNK = 64
EPS_LSH = 1e-6
LN_EPS = 1e-5

FH = 8
FD = 64
WINDOW = 512
INNER = 512
HID = 2730
QK_SCALE = 8.0
HALO = 513
TE = TOK + HALO
EXT = TOK + 512
HIDP = 2816
NHB = HIDP // 128

_CONST = {}


def host_constants():
    if _CONST:
        return _CONST
    import ml_dtypes
    import jax
    with jax.default_device(jax.devices('cpu')[0]):
        rot = np.asarray(jax.random.normal(
            jax.random.key(42), (16, DH, NHASH, NBUCK // 2), jax.numpy.float32))
    rot2 = np.zeros((NCORE, 128, NHASH * 512), np.float32)
    for c in range(NCORE):
        for hh in range(2):
            h = 2 * c + hh
            for n in range(NHASH):
                rot2[c, 64 * hh:64 * hh + 64, 512 * n:512 * n + 256] = rot[h, :, n, :]
                rot2[c, 64 * hh:64 * hh + 64, 512 * n + 256:512 * n + 512] = -rot[h, :, n, :]
    half = np.arange(0, FD, 2, dtype=np.float32)
    inv_freq = (1.0 / 10000.0 ** (half / FD)).astype(np.float32)
    t = np.arange(S, dtype=np.float32)
    freqs = np.concatenate([t[:, None] * inv_freq[None, :]] * 2, axis=1)
    base = ((half + 0.4 * FD) / (1.4 * FD)).astype(np.float32)
    power = ((t - S // 2) / float(WINDOW // 2)).astype(np.float32)
    scale = np.concatenate([base[None, :] ** power[:, None]] * 2, axis=1).astype(np.float32)
    cos, sin = np.cos(freqs).astype(np.float32), np.sin(freqs).astype(np.float32)
    _CONST.update(
        rot2=rot2.astype(ml_dtypes.bfloat16),
        csq=(cos * scale * QK_SCALE).astype(np.float32).T.copy(),
        snq=(sin * scale * QK_SCALE).astype(np.float32).T.copy(),
        csk=(cos / scale).astype(np.float32).T.copy(),
        snk=(sin / scale).astype(np.float32).T.copy(),
        maxscale=scale.max(axis=1).astype(np.float32).reshape(1, S),
    )
    return _CONST


class KB:
    def __init__(self, nc, stack, dev_dumps=()):
        self.nc = nc
        self.stack = stack
        self.dev = set(dev_dumps)
        self.t = {}
        self.sems = {}
        self.svals = {}
        for n in ("d0", "d1", "d2", "d3", "ve", "sc", "te", "gp", "cc", "ix"):
            self.sems[n] = stack.enter_context(nc.semaphore("sem_" + n))
            self.svals[n] = 0
        self.eng = {"sy": nc.sync, "sc": nc.scalar, "ve": nc.vector,
                    "te": nc.tensor, "gp": nc.gpsimd}
        self._used_names = set()

    def mk(self, ins, name, amt=1):
        ins.then_inc(self.sems[name], amt)
        self.svals[name] += amt
        return self.svals[name]

    def dma(self, q, out, in_, sem="d0", **kw):
        return self.mk(self.eng[q].dma_start(out=out, in_=in_, **kw), sem, 16)

    def w(self, waiter, name, val=None):
        self.eng[waiter].wait_ge(self.sems[name], self.svals[name] if val is None else val)

    def flush(self, e):
        self.eng[e].wait_ge(self.sems[e], self.svals[e])

    def dram_in(self, name, shape, dtype=F32):
        self.t[name] = self.nc.dram_tensor(name, list(shape), dtype, kind="ExternalInput")
        return self.t[name]

    def dram_out(self, name, shape, dtype=F32):
        self.t[name] = self.nc.dram_tensor(name, list(shape), dtype, kind="ExternalOutput")
        return self.t[name]

    def dram_tmp(self, name, shape, dtype=F32, **kw):
        self.t[name] = self.nc.dram_tensor(name, list(shape), dtype, kind="Internal", **kw)
        return self.t[name]

    def sbuf(self, name, shape, dtype=F32, stack=None):
        uname = name
        n = 0
        while uname in self._used_names:
            n += 1
            uname = f"{name}__{n}"
        self._used_names.add(uname)
        self.t[name] = (stack or self.stack).enter_context(
            self.nc.sbuf_tensor(uname, list(shape), dtype))
        return self.t[name]

    def phase_barrier(self):
        for e in ("sy", "sc", "ve", "te", "gp"):
            for n in ("d0", "d1", "d2", "ve", "sc", "te", "gp", "cc", "ix"):
                if self.svals[n] > 0:
                    self.w(e, n)

    def dump(self, tag, src, shape, dtype=F32):
        if tag not in self.dev:
            return
        d = self.dram_out("dev_" + tag, shape, dtype)
        self.dma("sy", d.ap(), src, sem="d3")
        self.w("sy", "d3")

    def declare(self):
        self.dram_in("xsl", [TOK, DM])
        self.dram_in("xT", [DM, S])
        self.dram_in("wqkT", [DM, 128])
        self.dram_in("wvT", [DM, 128])
        self.dram_in("rot2", [128, NHASH * 512], BF16)
        self.dram_in("xTe", [DM, TOK + HALO])
        self.dram_in("woutT", [DM, DM])
        self.dram_out("out", [DM, TOK])
        self.dram_tmp("stats_loc", [1, 2 * TOK])
        self.dram_tmp("stats_all", [NCORE, 2 * TOK])
        self.dram_in("fqkv", [2 * DM, 1536])
        self.dram_in("fwo", [2 * 512, DM])
        self.dram_in("fw1", [2 * DM, 2 * HIDP])
        self.dram_in("fw2", [2 * HIDP, DM])
        self.dram_in("fb1c", [2 * 128, 2 * NHB])
        self.dram_in("fb2c", [2 * 128, 8])
        self.dram_in("flgc", [2 * 128, 8])
        self.dram_in("flbc", [2 * 128, 8])
        self.dram_in("flgr", [2, DM])
        self.dram_in("flbr", [2, DM])
        self.dram_in("fqs", [128, 2])
        self.dram_in("fks", [128, 2])
        self.dram_in("fcsq", [FD, TOK])
        self.dram_in("fsnq", [FD, TOK])
        self.dram_in("fcsk", [FD, EXT])
        self.dram_in("fsnk", [FD, EXT])
        self.dram_in("hmask", [128, 1])
        self.dram_in("pvidx", [128, 8], U32)
        self.dram_tmp("ag_in", [DM, 514], BF16)
        self.dram_tmp("ag_out", [NCORE * DM, 514], BF16)
        self.PS = self.stack.enter_context(self.nc.psum_tensor("PS", [128, 3584], F32))
        self.PSb = self.stack.enter_context(self.nc.psum_tensor("PSb", [128, 1024], BF16))

    def phase_consts(self):
        nc = self.nc
        iota = self.sbuf("iota128", [128, 128])
        pidx = self.sbuf("pidx", [128, 1])
        identf = self.sbuf("identf", [128, 128])
        identb = self.sbuf("identb", [128, 128], BF16)
        onesf = self.sbuf("onesf", [1, 128])
        onesb = self.sbuf("onesb", [1, 128], BF16)
        onescol = self.sbuf("onescol", [128, 1], BF16)
        onescolf = self.sbuf("onescolf", [128, 1])
        onestile = self.sbuf("onestile", [128, 128])
        epscol = self.sbuf("epscol", [128, 1])
        m1col = self.sbuf("m1col", [128, 1])
        Lgt = self.sbuf("Lgt", [128, 128])
        m2U = self.sbuf("m2U", [128, 128], BF16)
        nc.vector.memset(epscol[:, :], LN_EPS)
        nc.vector.memset(m1col[:, :], -1.0)
        nc.vector.memset(onescolf[:, :], 1.0)
        nc.vector.memset(onestile[:, :], 1.0)
        nc.gpsimd.iota(iota[:, :], pattern=[[1, 128]], base=0, channel_multiplier=0,
                       allow_small_or_imprecise_dtypes=True)
        self.mk(nc.gpsimd.iota(pidx[:, :], pattern=[[0, 1]], base=0, channel_multiplier=1,
                               allow_small_or_imprecise_dtypes=True), "gp")
        self.w("ve", "gp")
        nc.vector.tensor_scalar(identf[:, :], iota[:, :], pidx[:, 0:1], None, op0=ALU.is_equal)
        nc.vector.tensor_scalar(identb[:, :], iota[:, :], pidx[:, 0:1], None, op0=ALU.is_equal)
        nc.vector.tensor_scalar(Lgt[:, :], iota[:, :], pidx[:, 0:1], None, op0=ALU.is_gt)
        pm64 = self.sbuf("pm64", [128, 1])
        identbS = self.sbuf("identbS", [128, 128], BF16)
        nc.vector.tensor_scalar(pm64[:, :], pidx[:, 0:1], 64.0, None, op0=ALU.subtract)
        nc.vector.tensor_scalar(m2U[:, :], iota[:, :], pidx[:, 0:1], -2.0,
                                op0=ALU.is_ge, op1=ALU.mult)
        nc.vector.memset(onesf[0:1, :], 1.0)
        nc.vector.memset(onesb[0:1, :], 1.0)
        self.mk(nc.vector.memset(onescol[:, :], 1.0), "ve")
        self.flush("ve")
        self.mk(nc.vector.tensor_scalar(identbS[:, :], iota[:, :], pm64[:, 0:1], None,
                                        op0=ALU.is_equal), "ve")
        self.flush("ve")
        for e in ("te", "sc", "gp", "sy"):
            self.w(e, "ve")

    def phase_stats(self):
        nc = self.nc
        from contextlib import ExitStack
        self._stats_stack = ExitStack()
        xa = self.sbuf("A_x", [128, 8 * DM], stack=self._stats_stack)
        sq = self.sbuf("A_sq", [128, DM], BF16, stack=self._stats_stack)
        col = self.sbuf("A_col", [128, 64], stack=self._stats_stack)
        rowb = self.sbuf("A_row", [32, 128], stack=self._stats_stack)
        xsl = self.t["xsl"].ap()
        PS = self.PS.ap()
        ve_tile = {}
        sc_tile = {}
        for j in range(16):
            b = j % 8
            if j >= 1:
                self.w("sy", "ve", ve_tile[j - 1])
                self.w("sy", "sc", sc_tile[j - 1])
            ld = self.dma("sy", xa[:, b * DM:(b + 1) * DM], xsl[j * 128:(j + 1) * 128, :])
            self.w("ve", "d0", ld)
            self.w("sc", "d0", ld)
            tile = xa[:, b * DM:(b + 1) * DM]
            ve_tile[j] = self.mk(nc.vector.reduce_sum(col[:, j:j + 1], tile, axis=AX.X), "ve")
            sc_tile[j] = self.mk(nc.scalar.activation(sq[:, :], tile, AF.Square,
                                                      accum_out=col[:, 16 + j:16 + j + 1]), "sc")
        if "xa" in self.dev:
            d0_ = self.dram_out("dev_xa", [128, 4 * DM])
            self.w("sy", "ve")
            self.w("sy", "sc")
            self.dma("sy", d0_.ap(), xa[:, :], sem="d3")
            self.w("sy", "d3")
        if "col0" in self.dev:
            d0_ = self.dram_out("dev_col0", [128, 32])
            self.w("sy", "ve")
            self.w("sy", "sc")
            self.dma("sy", d0_.ap(), col[:, 0:32], sem="d3")
            self.w("sy", "d3")
        self.w("ve", "sc")
        self.flush("ve")
        mu = col[:, 32:48]
        var = col[:, 48:64]
        nc.vector.tensor_scalar(mu, col[:, 0:16], 1.0 / DM, None, op0=ALU.mult)
        self.mk(nc.vector.tensor_scalar(var, col[:, 16:32], 1.0 / DM, None, op0=ALU.mult), "ve")
        self.flush("ve")
        tmp = col[:, 0:16]
        self.mk(nc.vector.tensor_mul(tmp, mu, mu), "ve")
        self.flush("ve")
        self.mk(nc.vector.tensor_sub(var, var, tmp), "ve")
        self.w("sc", "ve")
        self.mk(nc.scalar.activation(col[:, 16:32], var, AF.Ln, bias=self.t["epscol"][:, 0:1]), "sc")
        self.flush("sc")
        self.mk(nc.scalar.activation(col[:, 0:16], col[:, 16:32], AF.Exp, scale=-0.5), "sc")
        self.w("ve", "sc")
        rstd = col[:, 0:16]
        nmr = col[:, 16:32]
        self.mk(nc.vector.tensor_mul(nmr, mu, rstd), "ve")
        self.flush("ve")
        self.mk(nc.vector.tensor_scalar(nmr, nmr, -1.0, None, op0=ALU.mult), "ve")
        self.flush("ve")
        nc.vector.tensor_copy(col[:, 32:48], rstd)
        self.mk(nc.vector.tensor_copy(col[:, 48:64], nmr), "ve")
        self.w("te", "ve")
        self.mk(nc.tensor.transpose(PS[0:32, 0:128], col[:, 32:64], self.t["identf"][:, :]), "te")
        self.w("ve", "te")
        self.mk(nc.vector.tensor_copy(rowb[0:32, :], PS[0:32, 0:128]), "ve")
        self.w("sy", "ve")
        st = self.t["stats_loc"].ap()
        self.dma("sy", st[0:1, 0:TOK].rearrange("o (j p) -> (o j) p", p=128), rowb[0:16, :], sem="d1")
        self.dma("sy", st[0:1, TOK:2 * TOK].rearrange("o (j p) -> (o j) p", p=128), rowb[16:32, :], sem="d1")
        self.w("gp", "d1")
        self.mk(nc.gpsimd.collective_compute(
            "AllGather", ALU.bypass, replica_groups=[list(range(NCORE))],
            ins=[self.t["stats_loc"].ap().opt()], outs=[self.t["stats_all"].ap().opt()]), "cc")
        if "col" in self.dev:
            d = self.dram_out("dev_col", [128, 64])
            self.w("sy", "ve")
            self.dma("sy", d.ap(), col[:, :], sem="d3")
            d2 = self.dram_out("dev_rowb", [32, 128])
            self.dma("sy", d2.ap(), rowb[0:32, :], sem="d3")
            self.w("sy", "d3")
        self.w("sy", "cc")
        if "stats" in self.dev:
            d = self.dram_out("dev_stats", [NCORE, 2 * TOK])
            self.dma("sy", d.ap(), self.t["stats_all"].ap(), sem="d3")
            self.w("sy", "d3")
        self.phase_barrier()
        self._stats_stack.close()

    def phase_qkv(self):
        nc = self.nc
        WQ = 2048
        NW = S // WQ
        from contextlib import ExitStack
        self.ex_stack = ExitStack()
        self.sbuf("outfm", [128, S], BF16, stack=self.ex_stack)
        self.lsh_stack = ExitStack()
        self._qkv_stack = ExitStack()
        qkfm = self.sbuf("qkfm", [128, S], BF16, stack=self.lsh_stack)
        vfm = self.sbuf("vfm", [128, S], BF16, stack=self.lsh_stack)
        xw = self.sbuf("B_xw", [128, 2 * 8 * WQ], BF16, stack=self._qkv_stack)
        wq = self.sbuf("B_wq", [128, 8 * 128], BF16, stack=self._qkv_stack)
        wv = self.sbuf("B_wv", [128, 8 * 128], BF16, stack=self._qkv_stack)
        tmp = self.sbuf("B_tmp", [128, 2 * 512], stack=self._qkv_stack)
        brow = self.sbuf("B_brow", [128, 2 * 1024], stack=self._qkv_stack)
        cswcol = self.sbuf("B_cswcol", [128, 2], stack=self._qkv_stack)
        csrow = self.sbuf("B_csrow", [1, 256], BF16, stack=self._qkv_stack)
        srow = self.sbuf("B_srow", [1, 4096], stack=self._qkv_stack)
        PS = self.PS.ap()
        sa = self.t["stats_all"].ap()
        onesf, onesb, onescol = self.t["onesf"], self.t["onesb"], self.t["onescol"]

        for k in range(8):
            self.dma("gp", wq[:, k * 128:(k + 1) * 128],
                     self.t["wqkT"].ap()[k * 128:(k + 1) * 128, :], sem="d2")
            self.dma("gp", wv[:, k * 128:(k + 1) * 128],
                     self.t["wvT"].ap()[k * 128:(k + 1) * 128, :], sem="d2")
        self.w("te", "d2")
        for k in range(8):
            nc.tensor.matmul(PS[0:1, 3072:3200], onescol[:, 0:1], wq[:, k * 128:(k + 1) * 128],
                             start=(k == 0), stop=(k == 7))
        for k in range(8):
            ins = nc.tensor.matmul(PS[0:1, 3200:3328], onescol[:, 0:1],
                                   wv[:, k * 128:(k + 1) * 128], start=(k == 0), stop=(k == 7))
        self.mk(ins, "te")
        self.w("ve", "te")
        self.mk(nc.vector.tensor_copy(csrow[0:1, :], PS[0:1, 3072:3328]), "ve")
        self.w("te", "ve")
        nc.tensor.matmul(PS[:, 3072:3073], csrow[0:1, 0:128], onesb[0:1, 0:1], start=True, stop=True)
        self.mk(nc.tensor.matmul(PS[:, 3073:3074], csrow[0:1, 128:256], onesb[0:1, 0:1],
                                 start=True, stop=True), "te")
        self.w("ve", "te")
        self.mk(nc.vector.tensor_copy(cswcol[:, 0:2], PS[:, 3072:3074]), "ve")
        self.flush("ve")

        te_ch = {}
        ve_ch = {}
        te_win = {}
        sc_prev = [0]
        for w in range(NW):
            buf = (w % 2) * 8 * WQ
            sbuf_off = 0
            if w >= 2:
                self.w("gp", "te", te_win[w - 2])
            if w >= 1:
                self.w("sy", "te", te_win[w - 1])
            xT = self.t["xT"].ap()
            for k in range(8):
                self.dma("gp", xw[:, buf + k * WQ:buf + (k + 1) * WQ],
                         xT[k * 128:(k + 1) * 128, w * WQ:(w + 1) * WQ], sem="d0")
            self.dma("sy", srow[0:1, sbuf_off:sbuf_off + 2048], sa[w:w + 1, 0:TOK], sem="d1")
            self.dma("sy", srow[0:1, sbuf_off + 2048:sbuf_off + 4096],
                     sa[w:w + 1, TOK:2 * TOK], sem="d1")
            ld_tgt = self.svals["d0"]
            ld_s = self.svals["d1"]
            self.w("te", "d0", ld_tgt)
            self.w("te", "d1", ld_s)
            for cc in range(4):
                t0 = w * WQ + cc * 512
                pb = (cc % 2) * 1024
                qk_ps = PS[:, pb:pb + 512]
                v_ps = PS[:, pb + 512:pb + 1024]
                br_ps = PS[:, 2048:2560]
                bn_ps = PS[:, 2560:3072]
                gch = w * 4 + cc
                if gch >= 2:
                    self.w("te", "ve", ve_ch[gch - 2])
                if gch >= 1:
                    self.w("te", "sc", sc_prev[0])
                for k in range(8):
                    rhs = xw[:, buf + k * WQ + cc * 512:buf + k * WQ + cc * 512 + 512]
                    nc.tensor.matmul(qk_ps, wq[:, k * 128:(k + 1) * 128], rhs,
                                     start=(k == 0), stop=(k == 7))
                for k in range(8):
                    rhs = xw[:, buf + k * WQ + cc * 512:buf + k * WQ + cc * 512 + 512]
                    nc.tensor.matmul(v_ps, wv[:, k * 128:(k + 1) * 128], rhs,
                                     start=(k == 0), stop=(k == 7))
                co = sbuf_off + cc * 512
                nc.tensor.matmul(br_ps, onesf[0:1, :], srow[0:1, co:co + 512],
                                 start=True, stop=True)
                te_ch[gch] = self.mk(
                    nc.tensor.matmul(bn_ps, onesf[0:1, :],
                                     srow[0:1, 2048 + co:2048 + co + 512],
                                     start=True, stop=True), "te")
                self.w("sc", "te", te_ch[gch])
                bo = (cc % 2) * 1024
                nc.scalar.copy(brow[:, bo:bo + 512], br_ps)
                sc_t = self.mk(nc.scalar.copy(brow[:, bo + 512:bo + 1024], bn_ps), "sc")
                sc_prev[0] = sc_t
                self.w("ve", "te", te_ch[gch])
                self.w("ve", "sc", sc_t)
                nc.vector.tensor_mul(tmp[:, 0:512], qk_ps, brow[:, bo:bo + 512])
                self.mk(nc.vector.tensor_mul(tmp[:, 512:1024], v_ps, brow[:, bo:bo + 512]), "ve")
                self.flush("ve")
                nc.vector.scalar_tensor_tensor(
                    qkfm[:, t0:t0 + 512], brow[:, bo + 512:bo + 1024], cswcol[:, 0:1],
                    tmp[:, 0:512], op0=ALU.mult, op1=ALU.add)
                ve_ch[gch] = self.mk(nc.vector.scalar_tensor_tensor(
                    vfm[:, t0:t0 + 512], brow[:, bo + 512:bo + 1024], cswcol[:, 1:2],
                    tmp[:, 512:1024], op0=ALU.mult, op1=ALU.add), "ve")
            te_win[w] = self.svals["te"]
        if "qkfm" in self.dev:
            self.w("sy", "ve")
            d = self.dram_out("dev_qkfm", [128, S], BF16)
            d2 = self.dram_out("dev_vfm", [128, S], BF16)
            self.dma("sy", d.ap(), qkfm[:, :], sem="d3")
            self.dma("sy", d2.ap(), vfm[:, :], sem="d3")
            self.w("sy", "d3")
        self.phase_barrier()
        self._qkv_stack.close()

    def phase_buckets(self):
        nc = self.nc
        from contextlib import ExitStack
        self._bkt_stack = ExitStack()
        bkt = self.sbuf("bkt", [128, 1024], stack=self.lsh_stack)
        rot = self.sbuf("rotS", [128, NHASH * 512], BF16, stack=self._bkt_stack)
        mx = self.sbuf("mx", [128, 16], stack=self._bkt_stack)
        idxu = self.sbuf("idxu", [128, 64], U32, stack=self._bkt_stack)
        ld = self.dma("sc", rot[:, :], self.t["rot2"].ap(), sem="d2")
        qkfm = self.t["qkfm"]
        PS = self.PS.ap()
        self.w("te", "d2", ld)
        self.w("te", "ve")
        self.mk(nc.vector.memset(mx[:, :], 0.0), "ve")
        self.flush("ve")
        ve_half = {}
        for j in range(128):
            for hh in range(2):
                g = j * 2 + hh
                if g >= 1:
                    self.w("te", "ve", ve_half[g - 1])
                for n in range(NHASH):
                    ins = nc.tensor.matmul(
                        PS[:, n * 512:(n + 1) * 512],
                        qkfm.ap()[64 * hh:64 * hh + 64, 128 * j:128 * j + 128],
                        rot[64 * hh:64 * hh + 64, 512 * n:512 * n + 512],
                        start=True, stop=True)
                te_t = self.mk(ins, "te")
                self.w("ve", "te", te_t)
                self.mk(nc.vector.tensor_reduce(
                    mx[:, 0:4], PS[:, 0:2048].rearrange("p (u r) -> p u r", r=512),
                    axis=AX.X, op=ALU.max), "ve")
                self.flush("ve")
                for n in range(NHASH):
                    ins2 = nc.vector.max_index(idxu[:, n * 8:(n + 1) * 8], mx[:, n:n + 8],
                                               PS[:, n * 512:(n + 1) * 512])
                self.mk(ins2, "ve")
                self.flush("ve")
                ve_half[g] = self.mk(
                    nc.vector.tensor_copy(
                        bkt[:, j * 8 + hh * 4:j * 8 + hh * 4 + 4],
                        idxu[:, 0:32].rearrange("p (u k) -> p u k", k=8)[:, :, 0:1]), "ve")
        if "bkt" in self.dev:
            self.w("sy", "ve")
            d = self.dram_out("dev_bkt", [128, 1024])
            self.dma("sy", d.ap(), bkt[:, :], sem="d3")
            self.w("sy", "d3")
        self.phase_barrier()
        self._bkt_stack.close()


    def phase_pos(self):
        nc = self.nc
        from contextlib import ExitStack
        self._pos_stack = ExitStack()
        st = self._pos_stack
        PS = self.PS.ap()
        PSb = self.PSb.ap()
        bkt = self.t["bkt"]
        iota = self.t["iota128"]
        pidx = self.t["pidx"]
        onesf = self.t["onesf"]
        identb = self.t["identb"]
        m2U = self.t["m2U"]
        m1col = self.t["m1col"]
        Lgt = self.t["Lgt"]
        onestile = self.t["onestile"]
        onescolf = self.t["onescolf"]

        self.posu = []
        for u in range(8):
            self.posu.append(self.sbuf(f"posu{u}", [128, 128], U32, stack=self.lsh_stack))
        hi2A = self.sbuf("hi2A", [128, 128], stack=st)
        j2A = self.sbuf("j2A", [128, 128], stack=st)
        OHhi = self.sbuf("OHhi", [128, S], BF16, stack=st)
        OHlo = self.sbuf("OHlo", [128, 512], BF16, stack=st)
        Hs = self.sbuf("Hs", [128, 512], stack=st)
        cumS = self.sbuf("cumS", [128, 512], stack=st)
        Gsf = self.sbuf("Gsf", [128, 512], stack=st)
        Gi = self.sbuf("Gi_i32", [128, 512], mybir.dt.int32, stack=st)
        Gi2 = self.sbuf("Gi2_i32", [128, 512], mybir.dt.int32, stack=st)
        GsHi = self.sbuf("GsHi", [128, 512], BF16, stack=st)
        GsLo = self.sbuf("GsLo", [128, 512], BF16, stack=st)
        jrow = self.sbuf("jrow", [1, 4], stack=st)
        OHTs = self.sbuf("OHTs", [128, 512], BF16, stack=st)
        scrR = self.sbuf("scrR", [128, 256], BF16, stack=st)
        scrG = self.sbuf("scrG", [128, 16], stack=st)
        gcol = self.sbuf("gcol", [128, 128], stack=st)
        lrcol = self.sbuf("lrcol", [128, 128], stack=st)
        posf = self.sbuf("posf", [128, 128], stack=st)

        bview = bkt.ap().rearrange("p (j k) -> p j k", k=8)

        bi = self.sbuf("bi_i32", [128, 128], mybir.dt.int32, stack=st)
        bi2 = self.sbuf("bi2_i32", [128, 128], mybir.dt.int32, stack=st)
        for u in range(1 if "u0only" in self.dev else 8):
            self.mk(nc.vector.tensor_copy(bi[:, :], bview[:, :, u:u + 1]), "ve")
            self.flush("ve")
            nc.vector.tensor_scalar(bi2[:, :], bi[:, :], 127, None, op0=ALU.bitwise_and)
            self.mk(nc.vector.tensor_scalar(bi[:, :], bi[:, :], 7, None,
                                            op0=ALU.arith_shift_right), "ve")
            self.flush("ve")
            nc.vector.tensor_copy(hi2A[:, :], bi2[:, :])
            self.mk(nc.vector.tensor_copy(j2A[:, :], bi[:, :]), "ve")
            self.flush("ve")
            nc.vector.tensor_tensor(
                OHhi.ap()[:, :].rearrange("p (j i) -> p j i", i=128),
                hi2A.ap()[:, :].rearrange("p (j o) -> p j o", o=1).to_broadcast([128, 128, 128]),
                iota.ap()[:, :].rearrange("p (o i) -> p o i", o=1).to_broadcast([128, 128, 128]),
                op=ALU.is_equal)
            self.mk(nc.vector.tensor_tensor(
                OHlo.ap()[:, :].rearrange("p (j i) -> p j i", i=4),
                j2A.ap()[:, :].rearrange("p (j o) -> p j o", o=1).to_broadcast([128, 128, 4]),
                iota.ap()[:, 0:4].rearrange("p (o i) -> p o i", o=1).to_broadcast([128, 128, 4]),
                op=ALU.is_equal), "ve")
            self.flush("ve")
            self.mk(nc.vector.nop(), "ve")
            self.w("te", "ve")
            for B in range(128):
                ins = nc.tensor.matmul(PS[:, 4 * B:4 * B + 4],
                                       OHhi.ap()[:, 128 * B:128 * B + 128],
                                       OHlo.ap()[:, 4 * B:4 * B + 4],
                                       start=True, stop=True)
            self.mk(ins, "te")
            self.w("ve", "te")
            self.mk(nc.vector.tensor_copy(Hs[:, :], PS[:, 0:512]), "ve")
            self.flush("ve")
            for n2 in range(4):
                ins = nc.vector.tensor_tensor_scan(
                    cumS.ap()[:, n2:512:4],
                    onestile.ap()[:, 0:128],
                    Hs.ap()[:, n2:512:4],
                    0.0, op0=ALU.mult, op1=ALU.add)
            self.mk(ins, "ve")
            self.flush("ve")
            self.mk(nc.vector.tensor_sub(Gsf[:, :], cumS[:, :], Hs[:, :]), "ve")
            self.flush("ve")
            self.mk(nc.vector.nop(), "ve")
            self.w("te", "ve")
            tot = cumS[:, 508:512]
            nc.tensor.matmul(PS[:, 512:516], Lgt[:, :], tot, start=True, stop=False)
            self.mk(nc.tensor.matmul(PS[0:1, 1024:1028], onescolf[:, 0:1], tot,
                                     start=True, stop=True), "te")
            self.w("ve", "te")
            self.mk(nc.vector.tensor_copy(jrow[0:1, 0:4], PS[0:1, 1024:1028]), "ve")
            self.flush("ve")
            self.mk(nc.vector.nop(), "ve")
            self.w("te", "ve")
            for j2p in range(3):
                for tcol in range(j2p + 1, 4):
                    ins = nc.tensor.matmul(PS[:, 512 + tcol:513 + tcol], onesf[0:1, :],
                                           jrow[0:1, j2p:j2p + 1], start=False,
                                           stop=(j2p == 2 and tcol == 3))
            self.mk(ins, "te")
            self.w("ve", "te")
            self.mk(nc.vector.tensor_tensor(
                Gsf.ap().rearrange("p (b j) -> p b j", j=4),
                Gsf.ap().rearrange("p (b j) -> p b j", j=4),
                PS[:, 512:516].rearrange("p (o j) -> p o j", o=1).to_broadcast([128, 128, 4]),
                op=ALU.add), "ve")
            self.flush("ve")
            self.mk(nc.vector.tensor_copy(Gi[:, :], Gsf[:, :]), "ve")
            self.flush("ve")
            nc.vector.tensor_scalar(Gi2[:, :], Gi[:, :], 127, None, op0=ALU.bitwise_and)
            self.mk(nc.vector.tensor_scalar(Gi[:, :], Gi[:, :], -128, None,
                                            op0=ALU.bitwise_and), "ve")
            self.flush("ve")
            nc.vector.tensor_copy(GsHi[:, :], Gi[:, :])
            self.mk(nc.vector.tensor_copy(GsLo[:, :], Gi2[:, :]), "ve")
            self.flush("ve")
            self.mk(nc.vector.nop(), "ve")
            self.w("te", "ve")

            ve_b = {}
            sc_b = {}
            te_b = {}
            for B in range(128):
                par = B % 2
                psb_o = par * 256
                s2_o = 512 + par * 512
                vp_o = 1536 + par * 512
                if B >= 2:
                    self.w("te", "ve", ve_b[B - 2])
                    self.w("te", "sc", sc_b[B - 2])
                nc.tensor.transpose(PSb[:, psb_o:psb_o + 128],
                                    OHhi.ap()[:, 128 * B:128 * B + 128], identb[:, :])
                self.mk(nc.tensor.transpose(PSb[0:4, psb_o + 128:psb_o + 256],
                                            OHlo.ap()[:, 4 * B:4 * B + 4], identb[:, :]), "te")
                self.w("ve", "te")
                self.mk(nc.vector.tensor_copy(OHTs[:, psb_o:psb_o + 256],
                                              PSb[:, psb_o:psb_o + 256]), "ve")
                self.flush("ve")
                self.mk(nc.vector.nop(), "ve")
                evict_t = self.svals["ve"]
                self.w("te", "ve", evict_t)
                ohiT = OHTs[:, psb_o:psb_o + 128]
                oloT = OHTs[0:4, psb_o + 128:psb_o + 256]
                nc.tensor.matmul(PS[:, s2_o:s2_o + 128], ohiT, ohiT, start=True, stop=False)
                nc.tensor.matmul(PS[:, vp_o:vp_o + 4], ohiT, GsHi.ap()[:, 4 * B:4 * B + 4],
                                 start=True, stop=True)
                nc.tensor.matmul(PS[:, vp_o + 4:vp_o + 8], ohiT, GsLo.ap()[:, 4 * B:4 * B + 4],
                                 start=True, stop=True)
                nc.tensor.matmul(PS[:, s2_o:s2_o + 128], oloT, oloT, start=False, stop=False)
                te_b[B] = self.mk(nc.tensor.matmul(PS[:, s2_o:s2_o + 128], identb[:, :],
                                                   m2U[:, :], start=False, stop=True), "te")
                self.w("sc", "te", te_b[B])
                sc_b[B] = self.mk(nc.scalar.activation(
                    scrR[:, par * 128:par * 128 + 128], PS[:, s2_o:s2_o + 128], AF.Relu,
                    bias=m1col[:, 0:1], accum_out=lrcol[:, B:B + 1]), "sc")
                self.w("ve", "te", te_b[B])
                ve_b[B] = self.mk(nc.vector.scalar_tensor_tensor(
                    scrG[:, 0:8 if par == 0 else 16][:, -8:],
                    OHlo.ap()[:, 4 * B:4 * B + 4].rearrange("p (o j) -> p o j", o=1).to_broadcast([128, 2, 4]),
                    1.0,
                    PS[:, vp_o:vp_o + 8].rearrange("p (a j) -> p a j", j=4),
                    op0=ALU.mult, op1=ALU.mult,
                    accum_out=gcol[:, B:B + 1]), "ve")
            self.w("ve", "sc")
            self.flush("ve")
            self.mk(nc.vector.tensor_add(posf[:, :], gcol[:, :], lrcol[:, :]), "ve")
            self.flush("ve")
            self.mk(nc.vector.tensor_copy(self.posu[u].ap()[:, :], posf[:, :]), "ve")
            self.flush("ve")
        if "pos" in self.dev:
            self.w("sy", "ve")
            d = self.dram_out("dev_pos", [128, 8 * 128], U32)
            for u in range(1 if "u0only" in self.dev else 8):
                self.dma("sy", d.ap()[:, u * 128:(u + 1) * 128], self.posu[u].ap()[:, :], sem="d3")
            self.w("sy", "d3")
        if "posdbg" in self.dev:
            self.w("sy", "ve")
            self.w("sy", "sc")
            dd = self.dram_out("dev_posdbg", [128, 512 + 512 + 128 + 128 + 16384 // 128])
            self.dma("sy", dd.ap()[:, 0:512], Gsf[:, :], sem="d3")
            self.dma("sy", dd.ap()[:, 512:1024], Hs[:, :], sem="d3")
            self.dma("sy", dd.ap()[:, 1024:1152], gcol[:, :], sem="d3")
            self.dma("sy", dd.ap()[:, 1152:1280], lrcol[:, :], sem="d3")
            self.dma("sy", dd.ap()[:, 1280:1408], cumS[:, 0:128], sem="d3")
            self.w("sy", "d3")
        self.phase_barrier()
        self._pos_stack.close()


    def phase_scatter(self):
        nc = self.nc
        from contextlib import ExitStack
        self._sca_stack = ExitStack()
        st = self._sca_stack
        PSb = self.PSb.ap()
        qkfm, vfm = self.t["qkfm"], self.t["vfm"]
        for hh in range(2):
            self.dram_tmp(f"sorted{hh}", [NHASH * S, 256], BF16)
            self.dram_tmp(f"chout{hh}", [NHASH * S, 72], F32)
        pt = self.sbuf("ptile", [128, 4 * 256], BF16, stack=st)
        ssq = self.sbuf("ssq", [128, 8], stack=st)
        lnc_ = self.sbuf("lncol", [128, 8], stack=st)
        rstdk = self.sbuf("rstdk", [128, 8], stack=st)
        bcol = self.sbuf("bcol", [128, 8], stack=st)
        epsl = self.sbuf("epslsh", [128, 1], stack=st)
        sqscr = self.sbuf("sqscr", [128, 64], BF16, stack=st)
        ln8p = self.sbuf("ln8p", [128, 1], stack=st)
        ln8n = self.sbuf("ln8n", [128, 1], stack=st)
        nc.vector.memset(ln8p[:, :], float(np.log(8.0)))
        nc.vector.memset(ln8n[:, :], float(-np.log(8.0)))
        self.mk(nc.vector.memset(epsl[:, :], EPS_LSH), "ve")
        for b in range(4):
            nc.vector.memset(pt[:, b * 256 + 65:b * 256 + 66], 1.0)
            nc.vector.memset(pt[:, b * 256 + 144:b * 256 + 145], 1.0)
            self.mk(nc.vector.memset(pt[:, b * 256 + 145:b * 256 + 146], 0.0), "ve")
        self.flush("ve")
        identb = self.t["identb"]
        LN8 = float(np.log(8.0))
        ix_buf = {}
        ve_buf = {}
        for it, (j, hh) in enumerate([(j, hh) for j in range(128) for hh in range(2)]):
            b = it % 4
            po = b * 256
            u0 = hh * 4
            if it >= 4:
                self.w("ve", "ix", ix_buf[it - 4])
                self.w("te", "ix", ix_buf[it - 4])
            pb = 0
            if it >= 1:
                self.w("te", "ve", ve_buf[it - 1])
            idsl = self.t["identb"][0:64, 0:64] if hh == 0 else self.t["identbS"][64:128, 0:64]
            nc.tensor.transpose(PSb[:, pb:pb + 64],
                                qkfm.ap()[64 * hh:64 * hh + 64, 128 * j:128 * j + 128],
                                idsl)
            tq = self.mk(nc.tensor.transpose(PSb[:, pb + 64:pb + 128],
                                             vfm.ap()[64 * hh:64 * hh + 64, 128 * j:128 * j + 128],
                                             idsl), "te")
            self.w("ve", "te", tq)
            nc.vector.tensor_copy(pt[:, po:po + 64], PSb[:, pb:pb + 64])
            ve_buf[it] = self.mk(nc.vector.tensor_copy(pt[:, po + 160:po + 224],
                                                       PSb[:, pb + 64:pb + 128]), "ve")
            self.flush("ve")
            self.mk(nc.vector.nop(), "ve")
            self.w("sc", "ve")
            self.mk(nc.scalar.activation(sqscr[:, :], pt[:, po:po + 64],
                                         AF.Square, accum_out=ssq[:, b:b + 1]), "sc")
            self.flush("sc")
            self.mk(nc.scalar.activation(lnc_[:, b:b + 1], ssq[:, b:b + 1], AF.Ln,
                                         scale=1.0 / 64, bias=epsl[:, 0:1]), "sc")
            self.flush("sc")
            nc.scalar.activation(rstdk[:, b:b + 1], lnc_[:, b:b + 1], AF.Exp,
                                 scale=-0.5, bias=ln8n[:, 0:1])
            self.mk(nc.scalar.activation(bcol[:, b:b + 1], lnc_[:, b:b + 1], AF.Exp,
                                         scale=0.5, bias=ln8p[:, 0:1]), "sc")
            self.w("ve", "sc")
            nc.vector.tensor_scalar(pt[:, po + 80:po + 144], pt[:, po:po + 64],
                                    rstdk[:, b:b + 1], None, op0=ALU.mult)
            nc.vector.tensor_scalar(pt[:, po + 64:po + 65], bcol[:, b:b + 1], -1.0, None,
                                    op0=ALU.mult)
            self.mk(nc.vector.tensor_scalar(pt[:, po + 224:po + 225], bcol[:, b:b + 1], -1.0,
                                            None, op0=ALU.mult), "ve")
            self.flush("ve")
            self.mk(nc.vector.nop(), "ve")
            self.w("gp", "ve")
            srt = self.t[f"sorted{hh}"].ap()
            for n in range(NHASH):
                ins = nc.gpsimd.indirect_dma_start(
                    out=srt[:, :],
                    out_offset=IndirectOffsetOnAxis(ap=self.posu[u0 + n].ap()[:, j:j + 1], axis=0),
                    in_=pt[:, po:po + 256],
                    in_offset=None,
                    element_offset=n * S * 256)
                self.mk(ins, "ix", 16)
            ix_buf[it] = self.svals["ix"]
        self.w("gp", "ix")
        self.phase_barrier()
        self._sca_stack.close()

    def phase_chunk(self):
        import dataclasses
        nc = self.nc
        from contextlib import ExitStack
        self._chk_stack = ExitStack()
        st = self._chk_stack
        PS = self.PS.ap()
        qw = self.sbuf("C_qw", [128, 2 * 512], BF16, stack=st)
        kw = self.sbuf("C_kw", [128, 2 * 640], BF16, stack=st)
        vt = self.sbuf("C_vt", [128, 2 * 536], BF16, stack=st)
        PT = self.sbuf("C_pt", [128, 2 * 512], BF16, stack=st)
        chb = self.sbuf("C_chb", [128, 2 * 288], stack=st)
        exq = self.sbuf("C_exq", [128, 2 * 4], stack=st)
        for b in range(2):
            for g in range(8):
                nc.vector.memset(vt[:, b * 536 + g * 67:b * 536 + g * 67 + 1], 1.0)
        self.mk(nc.vector.nop(), "ve")
        self.flush("ve")
        NB = NHASH * S // 512
        if "chk1blk" in self.dev:
            NB = 1
        ve_blk = {}
        sc_blk = {}
        te_blk = {}
        dv_blk = {}
        d2_blk = {}
        for hh in range(1 if "chk1blk" in self.dev else 2):
            srt = self.t[f"sorted{hh}"].ap()
            cho = self.t[f"chout{hh}"].ap()
            for blk in range(NB):
                it = hh * NB + blk
                b = it % 2
                pos0 = blk * 512
                bo5 = b * 536
                if it >= 2:
                    self.w("sy", "te", te_blk[it - 2])
                    self.w("gp", "te", te_blk[it - 2])
                    self.w("gp", "sc", sc_blk[it - 2])
                    self.w("sy", "ve", ve_blk[it - 2])
                self.dma("sy", qw[0:128, b * 512:b * 512 + 512],
                         srt[pos0:pos0 + 512, 0:128], sem="d0", transpose=True)
                if blk == 0:
                    self.dma("sy", kw[0:128, b * 640:b * 640 + 128],
                             srt[NHASH * S - 128:NHASH * S, 80:208], sem="d0", transpose=True)
                    self.dma("sy", kw[0:128, b * 640 + 128:b * 640 + 640],
                             srt[pos0:pos0 + 512, 80:208], sem="d0", transpose=True)
                else:
                    self.dma("sy", kw[0:128, b * 640:b * 640 + 640],
                             srt[pos0 - 128:pos0 + 512, 80:208], sem="d0", transpose=True)
                vdst = vt.ap()[:, bo5:bo5 + 536].rearrange("p (g c) -> p g c", c=67)
                if blk == 0:
                    self.dma("gp", vdst[0:64, 0:1, 1:66],
                             srt[NHASH * S - 64:NHASH * S, 160:225], sem="d1")
                    self.dma("gp", vdst[64:128, 0:1, 1:66],
                             srt[0:64, 160:225], sem="d1")
                    src0 = srt[0:1, 0:1]
                    ap7 = dataclasses.replace(
                        src0, offset=0 * 256 + 160,
                        ap=[[256, 128], [64 * 256, 7], [1, 65]])
                    self.dma("gp", vdst[:, 1:8, 1:66], ap7, sem="d1")
                else:
                    src0 = srt[0:1, 0:1]
                    ap8 = dataclasses.replace(
                        src0, offset=(pos0 - 64) * 256 + 160,
                        ap=[[256, 128], [64 * 256, 8], [1, 65]])
                    self.dma("gp", vdst[:, 0:8, 1:66], ap8, sem="d1")
                dtgt0 = self.svals["d0"]
                dtgt1 = self.svals["d1"]
                dv_blk[it] = dtgt1
                self.w("sc", "d1", dtgt1)
                if it >= 2:
                    self.w("sc", "ve", ve_blk[it - 2])
                sc_e = self.mk(nc.scalar.activation(
                    exq[:, b * 4:b * 4 + 4],
                    vt.ap()[:, bo5 + 132:bo5 + 536:134],
                    AF.Exp, scale=-1.0), "sc")
                self.w("te", "d0", dtgt0)
                if it >= 1:
                    self.w("te", "sc", sc_blk[it - 1])
                for cc in range(8):
                    sreg = PS[:, (cc // 4) * 512 + (cc % 4) * 64:
                              (cc // 4) * 512 + (cc % 4) * 64 + 64]
                    ins = nc.tensor.matmul(
                        sreg,
                        kw[0:66, b * 640 + 64 * cc + 64:b * 640 + 64 * cc + 192],
                        qw[0:66, b * 512 + 64 * cc:b * 512 + 64 * cc + 64],
                        start=True, stop=True)
                te_s = self.mk(ins, "te")
                self.w("sc", "te", te_s)
                nc.scalar.activation(PT[:, b * 512:b * 512 + 256], PS[:, 0:256], AF.Exp)
                sc_blk[it] = self.mk(nc.scalar.activation(PT[:, b * 512 + 256:b * 512 + 512],
                                                          PS[:, 512:768], AF.Exp), "sc")
                self.w("te", "sc", sc_blk[it])
                self.w("te", "d1", dtgt1)
                if it >= 1:
                    self.w("te", "ve", ve_blk[it - 1])
                for cc in range(8):
                    pr = cc // 2
                    qb = (cc % 2) * 64
                    ins = nc.tensor.matmul(
                        PS[qb:qb + 64, 1024 + pr * 128:1024 + pr * 128 + 65],
                        PT[:, b * 512 + cc * 64:b * 512 + cc * 64 + 64],
                        vt.ap()[:, bo5 + 67 * cc:bo5 + 67 * cc + 65],
                        start=True, stop=True)
                te_blk[it] = self.mk(ins, "te")
                self.w("ve", "te", te_blk[it])
                self.w("ve", "sc", sc_e)
                if it >= 2:
                    self.w("ve", "d2", d2_blk[it - 2])
                for pr in range(4):
                    ins = nc.vector.tensor_scalar(
                        chb[:, b * 288 + pr * 72:b * 288 + pr * 72 + 65],
                        PS[:, 1024 + pr * 128:1024 + pr * 128 + 65],
                        exq[:, b * 4 + pr:b * 4 + pr + 1], None, op0=ALU.mult)
                ve_blk[it] = self.mk(ins, "ve")
                self.w("sy", "ve", ve_blk[it])
                self.dma("sy", cho[pos0:pos0 + 512, :].rearrange("(g p) c -> p g c", p=128),
                         chb[:, b * 288:b * 288 + 288].rearrange("p (g c) -> p g c", c=72),
                         sem="d2")
                d2_blk[it] = self.svals["d2"]
        self.w("sy", "d2")
        if "chkdbg" in self.dev:
            self.phase_barrier()
            d = self.dram_out("dev_qw", [128, 512], BF16)
            self.dma("sy", d.ap(), qw[:, 0:512], sem="d3")
            d = self.dram_out("dev_kw", [128, 640], BF16)
            self.dma("sy", d.ap(), kw[:, 0:640], sem="d3")
            d = self.dram_out("dev_PT", [128, 512], BF16)
            self.dma("sy", d.ap(), PT[:, 0:512], sem="d3")
            d = self.dram_out("dev_vt", [128, 536], BF16)
            self.dma("sy", d.ap(), vt[:, 0:536], sem="d3")
            d = self.dram_out("dev_exq", [128, 4])
            self.dma("sy", d.ap(), exq[:, 0:4], sem="d3")
            self.w("sy", "d3")
        self.phase_barrier()
        self._chk_stack.close()

    def phase_combine(self):
        nc = self.nc
        from contextlib import ExitStack
        self._cmb_stack = ExitStack()
        st = self._cmb_stack
        PSb = self.PSb.ap()
        outfm = self.t["outfm"]
        gb = self.sbuf("G_gb", [128, 4 * 4 * 72], stack=st)
        sm = self.sbuf("G_sm", [128, 4 * 72], stack=st)
        ot = self.sbuf("G_ot", [128, 4 * 64], BF16, stack=st)
        identb = self.t["identb"]
        ix_it = {}
        ve_it = {}
        te_it = {}
        for it, (j, hh) in enumerate([(j, hh) for j in range(128) for hh in range(2)]):
            b = it % 4
            u0 = hh * 4
            cho = self.t[f"chout{hh}"].ap()
            if it >= 4:
                self.w("gp", "ve", ve_it[it - 4])
            for n in range(NHASH):
                ins = nc.gpsimd.indirect_dma_start(
                    out=gb[:, b * 288 + n * 72:b * 288 + n * 72 + 65],
                    out_offset=None,
                    in_=cho[:, :],
                    in_offset=IndirectOffsetOnAxis(ap=self.posu[u0 + n].ap()[:, j:j + 1], axis=0),
                    element_offset=n * S * 72)
                self.mk(ins, "ix", 16)
            ix_it[it] = self.svals["ix"]
            if it == 255:
                ins = nc.gpsimd.indirect_dma_start(
                    out=gb[:, b * 288:b * 288 + 65], out_offset=None,
                    in_=cho[:, :],
                    in_offset=IndirectOffsetOnAxis(ap=self.posu[u0].ap()[:, j:j + 1],
                                                   axis=0),
                    element_offset=0 * S * 72)
                self.mk(ins, "ix", 16)
            self.w("ve", "ix", ix_it[it] + 16)
            go = b * 288
            nc.vector.tensor_add(gb[:, go:go + 65], gb[:, go:go + 65], gb[:, go + 72:go + 137])
            self.mk(nc.vector.tensor_add(gb[:, go + 144:go + 209], gb[:, go + 144:go + 209],
                                         gb[:, go + 216:go + 281]), "ve")
            self.flush("ve")
            self.mk(nc.vector.tensor_add(sm[:, b * 72:b * 72 + 65], gb[:, go:go + 65],
                                         gb[:, go + 144:go + 209]), "ve")
            self.flush("ve")
            self.mk(nc.vector.reciprocal(sm[:, b * 72 + 66:b * 72 + 67],
                                         sm[:, b * 72:b * 72 + 1]), "ve")
            self.flush("ve")
            ve_it[it] = self.mk(nc.vector.tensor_scalar(
                ot[:, b * 64:b * 64 + 64], sm[:, b * 72 + 1:b * 72 + 65],
                sm[:, b * 72 + 66:b * 72 + 67], None, op0=ALU.mult), "ve")
            self.flush("ve")
            self.mk(nc.vector.nop(), "ve")
            self.w("te", "ve", self.svals["ve"])
            if it >= 2:
                self.w("te", "ve", ve_it[it - 2])
            if it >= 1:
                self.w("te", "ve", ve_it[it - 1])
            ins = nc.tensor.transpose(PSb[0:64, 0:128],
                                      ot[:, b * 64:b * 64 + 64], identb[:, :])
            te_it[it] = self.mk(ins, "te")
            self.w("ve", "te", te_it[it])
            ve_it[it] = self.mk(nc.vector.tensor_copy(
                outfm.ap()[64 * hh:64 * hh + 64, 128 * j:128 * j + 128],
                PSb[0:64, 0:128]), "ve")
        self.phase_barrier()
        self._cmb_stack.close()
        self.lsh_stack.close()
        if "outfm" in self.dev:
            d = self.dram_out("dev_outfm", [128, S], BF16)
            self.dma("sy", d.ap(), outfm.ap()[:, :], sem="d3")
            self.w("sy", "d3")


    def phase_exchange(self):
        nc = self.nc
        TE_ = TOK + HALO
        self.dram_tmp("a2a_in", [NCORE * 128, TE_], BF16)
        self.dram_tmp("a2a_out", [NCORE * 128, TE_], BF16)
        outfm = self.t["outfm"]
        ain = self.t["a2a_in"].ap()
        for r in range(NCORE):
            lo = 2048 * r - HALO
            if lo < 0:
                zt = self.sbuf("zhalo", [128, HALO], BF16, stack=self.ex_stack)
                zv = self.mk(nc.vector.memset(zt[:, 0:HALO], 0.0), "ve")
                self.flush("ve")
                self.w("sy", "ve", zv)
                self.dma("sy", ain[r * 128:(r + 1) * 128, 0:HALO],
                         zt[:, 0:HALO], sem="d0")
                self.dma("sy", ain[r * 128:(r + 1) * 128, HALO:TE_],
                         outfm.ap()[:, 0:2048], sem="d0")
            else:
                self.dma("sy", ain[r * 128:(r + 1) * 128, :],
                         outfm.ap()[:, lo:lo + TE_], sem="d0")
        self.w("gp", "d0")
        from concourse.aot_env import allow_ici_all_to_all
        with allow_ici_all_to_all():
            self.mk(nc.gpsimd.collective_compute(
                "AllToAll", ALU.bypass, replica_groups=[list(range(NCORE))],
                ins=[self.t["a2a_in"].ap().opt()], outs=[self.t["a2a_out"].ap().opt()]), "cc")
        self.w("sy", "cc")
        self.w("te", "cc")

    def phase_x1(self):
        nc = self.nc
        TE_ = TOK + HALO
        self.phase_barrier()
        self.ex_stack.close()
        from contextlib import ExitStack
        self._x1_stack = ExitStack()
        st = self._x1_stack
        PS = self.PS.ap()
        xf = self.sbuf("xf", [128, 8 * TE_])
        self.dram_tmp("x1de", [DM, TE_])
        woS = self.sbuf("D_wo", [128, 64 * 128], BF16, stack=st)
        self.dram_tmp("aod", [DM, TE_], BF16)
        aob = self.sbuf("D_ao", [128, 2 * 512], BF16, stack=st)
        hdb = self.sbuf("D_hd", [128, 8 * 512], BF16, stack=st)
        sqt = self.sbuf("D_sq", [128, 2 * 512], BF16, stack=st)
        rowz = self.sbuf("D_rowz", [1, 6 * TE_], stack=st)
        x1b = self.sbuf("D_x1b", [128, 2 * 512], stack=st)
        scr = self.sbuf("D_scr", [128, 2 * 512], stack=st)
        onesf = self.t["onesf"]
        onescol = self.t["onescol"]
        onescolf = self.t["onescolf"]
        epscol = self.t["epscol"]
        aout = self.t["a2a_out"].ap()
        xTe = self.t["xTe"].ap()
        woutT = self.t["woutT"].ap()
        x1de = self.t["x1de"].ap()
        tchunks = [(i * 512, min(512, TE_ - i * 512)) for i in range((TE_ + 511) // 512)]

        for m in range(8):
            for k in range(8):
                self.dma("gp", woS[:, (m * 8 + k) * 128:(m * 8 + k) * 128 + 128],
                         woutT[k * 128:(k + 1) * 128, m * 128:(m + 1) * 128], sem="d2")
        wo_ld = self.svals["d2"]
        self.w("te", "d2", wo_ld)

        ve_p = {}
        d0v1 = {}
        for ti, (t0, tl) in enumerate(tchunks):
            if ti > 0:
                self.w("te", "ve", self.svals["ve"])
            for k in range(8):
                self.dma("sc", hdb[:, k * 512:k * 512 + tl],
                         aout[k * 128:(k + 1) * 128, t0:t0 + tl], sem="d1")
            ld_h = self.svals["d1"]
            self.w("te", "d1", ld_h)
            for m in range(8):
                mm_ps = PS[:, (m % 2) * 512:(m % 2) * 512 + 512]
                key = ti * 8 + m
                if key >= 2:
                    self.w("te", "ve", ve_p[key - 2])
                for k in range(8):
                    ins = nc.tensor.matmul(mm_ps[:, 0:tl],
                                           woS[:, (m * 8 + k) * 128:(m * 8 + k) * 128 + 128],
                                           hdb[:, k * 512:k * 512 + tl],
                                           start=(k == 0), stop=(k == 7))
                te_m = self.mk(ins, "te")
                self.w("ve", "te", te_m)
                b2 = m % 2
                if key >= 2:
                    self.w("ve", "d0", d0v1[key - 2])
                ve_evict = self.mk(nc.vector.tensor_copy(
                    aob[:, b2 * 512:b2 * 512 + tl], mm_ps[:, 0:tl]), "ve")
                self.w("sc", "ve", ve_evict)
                sc_sq = self.mk(nc.scalar.activation(sqt[:, b2 * 512:b2 * 512 + tl],
                                                     aob[:, b2 * 512:b2 * 512 + tl],
                                                     AF.Square), "sc")
                self.w("sy", "ve", ve_evict)
                self.dma("sy", self.t["aod"].ap()[m * 128:(m + 1) * 128, t0:t0 + tl],
                         aob[:, b2 * 512:b2 * 512 + tl], sem="d0")
                d0v1[key] = self.svals["d0"]
                self.w("te", "ve", ve_evict)
                self.w("te", "sc", sc_sq)
                ve_p[key] = ve_evict
                nc.tensor.matmul(PS[0:1, 3072:3072 + tl], onescol[:, 0:1],
                                 aob[:, b2 * 512:b2 * 512 + tl],
                                 start=(m == 0), stop=(m == 7))
                te_q = self.mk(nc.tensor.matmul(PS[0:1, 2560:2560 + tl], onescol[:, 0:1],
                                                sqt[:, b2 * 512:b2 * 512 + tl],
                                                start=(m == 0), stop=(m == 7)),
                               "te")
            self.w("ve", "te", te_q)
            nc.vector.tensor_copy(rowz[0:1, t0:t0 + tl], PS[0:1, 3072:3072 + tl])
            self.mk(nc.vector.tensor_copy(rowz[0:1, TE_ + t0:TE_ + t0 + tl],
                                          PS[0:1, 2560:2560 + tl]), "ve")
        self.flush("ve")
        self._ln_rows(rowz, 0, TE_)
        ve_p2 = {}
        te_p2 = {}
        d0v2 = {}
        self.w("sc", "d0", self.svals["d0"])
        for ti, (t0, tl) in enumerate(tchunks):
            br_ps = PS[:, 2048:2560]
            bn_ps = PS[:, 1024:1536]
            self.w("te", "ve", self.svals["ve"])
            nc.tensor.matmul(br_ps[:, 0:tl], onesf[0:1, :],
                             rowz[0:1, 2 * TE_ + t0:2 * TE_ + t0 + tl], start=True, stop=True)
            te_b = self.mk(nc.tensor.matmul(bn_ps[:, 0:tl], onesf[0:1, :],
                                            rowz[0:1, 3 * TE_ + t0:3 * TE_ + t0 + tl],
                                            start=True, stop=True), "te")
            for m in range(8):
                b2 = m % 2
                key = ti * 8 + m
                if key >= 2:
                    self.w("sc", "ve", ve_p2[key - 2])
                    self.w("sc", "te", te_p2[key - 2])
                    self.w("sc", "d0", d0v2[key - 2])
                ld_x = self.dma("sc", x1b[:, b2 * 512:b2 * 512 + tl],
                                xTe[m * 128:(m + 1) * 128, t0:t0 + tl], sem="d1")
                self.dma("sc", aob[:, b2 * 512:b2 * 512 + tl],
                         self.t["aod"].ap()[m * 128:(m + 1) * 128, t0:t0 + tl], sem="d1")
                ld_x = self.svals["d1"]
                self.w("ve", "d1", ld_x)
                self.w("ve", "te", te_b)
                nc.vector.tensor_mul(scr[:, b2 * 512:b2 * 512 + tl],
                                     aob[:, b2 * 512:b2 * 512 + tl], br_ps[:, 0:tl])
                self.mk(nc.vector.nop(), "ve")
                self.flush("ve")
                nc.vector.tensor_add(scr[:, b2 * 512:b2 * 512 + tl],
                                     scr[:, b2 * 512:b2 * 512 + tl], bn_ps[:, 0:tl])
                self.mk(nc.vector.nop(), "ve")
                self.flush("ve")
                ve_x = self.mk(nc.vector.tensor_add(x1b[:, b2 * 512:b2 * 512 + tl],
                                                    x1b[:, b2 * 512:b2 * 512 + tl],
                                                    scr[:, b2 * 512:b2 * 512 + tl]), "ve")
                self.flush("ve")
                self.w("sy", "ve", ve_x)
                self.dma("sy", x1de[m * 128:(m + 1) * 128, t0:t0 + tl],
                         x1b[:, b2 * 512:b2 * 512 + tl], sem="d0")
                d0v2[key] = self.svals["d0"]
                self.w("sc", "ve", ve_x)
                sc_sq = self.mk(nc.scalar.activation(sqt[:, b2 * 512:b2 * 512 + tl],
                                                     x1b[:, b2 * 512:b2 * 512 + tl],
                                                     AF.Square), "sc")
                self.w("te", "ve", ve_x)
                self.w("te", "sc", sc_sq)
                nc.tensor.matmul(PS[0:1, 3072:3072 + tl], onescolf[:, 0:1],
                                 x1b[:, b2 * 512:b2 * 512 + tl], start=(m == 0), stop=(m == 7))
                te_q = self.mk(nc.tensor.matmul(PS[0:1, 2560:2560 + tl], onescol[:, 0:1],
                                                sqt[:, b2 * 512:b2 * 512 + tl],
                                                start=(m == 0), stop=(m == 7)), "te")
                ve_p2[key] = ve_x
                te_p2[key] = te_q
            self.w("ve", "te", te_q)
            nc.vector.tensor_copy(rowz[0:1, t0:t0 + tl], PS[0:1, 3072:3072 + tl])
            self.mk(nc.vector.tensor_copy(rowz[0:1, TE_ + t0:TE_ + t0 + tl],
                                          PS[0:1, 2560:2560 + tl]), "ve")
            self.flush("ve")
        self._ln_rows(rowz, 0, TE_)
        ve_p3 = {}
        self.w("sc", "d0", self.svals["d0"])
        for ti, (t0, tl) in enumerate(tchunks):
            br_ps = PS[:, 2048:2560]
            bn_ps = PS[:, 1024:1536]
            self.w("te", "ve", self.svals["ve"])
            nc.tensor.matmul(br_ps[:, 0:tl], onesf[0:1, :],
                             rowz[0:1, 2 * TE_ + t0:2 * TE_ + t0 + tl], start=True, stop=True)
            te_b = self.mk(nc.tensor.matmul(bn_ps[:, 0:tl], onesf[0:1, :],
                                            rowz[0:1, 3 * TE_ + t0:3 * TE_ + t0 + tl],
                                            start=True, stop=True), "te")
            for m in range(8):
                b2 = m % 2
                key = ti * 8 + m
                if key >= 2:
                    self.w("sc", "ve", ve_p3[key - 2])
                ld_x = self.dma("sc", x1b[:, b2 * 512:b2 * 512 + tl],
                                x1de[m * 128:(m + 1) * 128, t0:t0 + tl], sem="d1")
                self.w("ve", "d1", ld_x)
                self.w("ve", "te", te_b)
                ve_m3 = self.mk(nc.vector.tensor_mul(scr[:, b2 * 512:b2 * 512 + tl],
                                                     x1b[:, b2 * 512:b2 * 512 + tl],
                                                     br_ps[:, 0:tl]), "ve")
                ve_p3[key] = ve_m3
                self.flush("ve")
                self.mk(nc.vector.tensor_add(xf.ap()[:, m * TE_ + t0:m * TE_ + t0 + tl],
                                             scr[:, b2 * 512:b2 * 512 + tl], bn_ps[:, 0:tl]),
                        "ve")
                self.flush("ve")
        self.phase_barrier()
        self._x1_stack.close()
        if "xf" in self.dev:
            d = self.dram_out("dev_xf", [128, 8 * TE_])
            self.dma("sy", d.ap(), xf.ap()[:, :], sem="d3")
            self.w("sy", "d3")
        if "aod" in self.dev:
            d = self.dram_out("dev_aod", [DM, TE_], BF16)
            self.dma("sy", d.ap(), self.t["aod"].ap()[:, :], sem="d3")
            self.w("sy", "d3")
        if "x1de" in self.dev:
            d = self.dram_out("dev_x1de", [DM, TE_])
            self.dma("sy", d.ap(), x1de[:, :], sem="d3")
            self.w("sy", "d3")

    def _ln_rows(self, rowz, o, TE_):
        nc = self.nc
        mu = rowz[0:1, 4 * TE_:5 * TE_]
        var = rowz[0:1, 5 * TE_:6 * TE_]
        scr2 = rowz[0:1, 2 * TE_:3 * TE_]
        nc.vector.tensor_scalar(mu, rowz[0:1, 0:TE_], 1.0 / DM, None, op0=ALU.mult)
        self.mk(nc.vector.tensor_scalar(var, rowz[0:1, TE_:2 * TE_], 1.0 / DM, None,
                                        op0=ALU.mult), "ve")
        self.flush("ve")
        self.mk(nc.vector.tensor_mul(scr2, mu, mu), "ve")
        self.flush("ve")
        self.mk(nc.vector.tensor_sub(var, var, scr2), "ve")
        self.w("sc", "ve")
        self.mk(nc.scalar.activation(var, var, AF.Ln, bias=self.t["epscol"][0:1, 0:1]), "sc")
        self.flush("sc")
        self.mk(nc.scalar.activation(scr2, var, AF.Exp, scale=-0.5), "sc")
        self.w("ve", "sc")
        self.mk(nc.vector.tensor_mul(rowz[0:1, 3 * TE_:4 * TE_], mu, scr2), "ve")
        self.flush("ve")
        self.mk(nc.vector.tensor_scalar(rowz[0:1, 3 * TE_:4 * TE_], rowz[0:1, 3 * TE_:4 * TE_],
                                        -1.0, None, op0=ALU.mult), "ve")
        self.flush("ve")


    def phase_fconsts(self):
        nc = self.nc
        iota = self.t["iota128"]
        pidx = self.t["pidx"]
        Rs = self.sbuf("Rsign", [128, 128], BF16)
        tri_le = self.sbuf("tri_le", [128, 128], BF16)
        tri_ge = self.sbuf("tri_ge", [128, 128], BF16)
        toQ4 = self.sbuf("trioffQ4", [128, 128], BF16)
        toQ0 = self.sbuf("trioffQ0", [128, 128], BF16)
        H0 = self.sbuf("H0", [128, 1152], BF16)
        hm = self.sbuf("hmcol", [128, 1])
        hm1 = self.sbuf("hm1col", [128, 1])
        from contextlib import ExitStack
        fcs = ExitStack()
        sc1 = self.sbuf("FC_s1", [128, 128], stack=fcs)
        sc2 = self.sbuf("FC_s2", [128, 128], stack=fcs)
        i32a = self.sbuf("FC_i32a", [128, 128], mybir.dt.int32, stack=fcs)
        i32b = self.sbuf("FC_i32b", [128, 1], mybir.dt.int32, stack=fcs)
        def V(ins):
            self.mk(ins, "ve")
            self.flush("ve")
        V(nc.vector.tensor_scalar(sc1[:, :], iota[:, :], pidx[:, 0:1], None,
                                  op0=ALU.subtract))
        V(nc.vector.tensor_scalar(sc2[:, :], sc1[:, :], 32.0, None,
                                  op0=ALU.is_equal))
        V(nc.vector.tensor_scalar(sc1[:, :], sc1[:, :], -32.0, None,
                                  op0=ALU.is_equal))
        V(nc.vector.tensor_sub(sc2[:, :], sc2[:, :], sc1[:, :]))
        V(nc.vector.tensor_copy(i32a[:, :], iota[:, :]))
        V(nc.vector.tensor_scalar(i32a[:, :], i32a[:, :], 6, None,
                                  op0=ALU.arith_shift_right))
        V(nc.vector.tensor_copy(i32b[:, :], pidx[:, :]))
        V(nc.vector.tensor_scalar(i32b[:, :], i32b[:, :], 6, None,
                                  op0=ALU.arith_shift_right))
        V(nc.vector.tensor_copy(sc1[:, :], i32a[:, :]))
        pcol = self.sbuf("FC_pcol", [128, 1], stack=fcs)
        V(nc.vector.tensor_copy(pcol[:, :], i32b[:, :]))
        V(nc.vector.tensor_scalar(sc1[:, :], sc1[:, :], pcol[:, 0:1], None,
                                  op0=ALU.is_equal))
        V(nc.vector.tensor_mul(Rs[:, :], sc2[:, :], sc1[:, :]))
        nc.vector.tensor_scalar(tri_le[:, :], iota[:, :], pidx[:, 0:1], None,
                                op0=ALU.is_ge)
        nc.vector.tensor_scalar(tri_ge[:, :], iota[:, :], pidx[:, 0:1], None,
                                op0=ALU.is_le)
        V(nc.vector.tensor_scalar(sc1[:, :], iota[:, :], pidx[:, 0:1], None,
                                   op0=ALU.is_lt))
        V(nc.vector.tensor_scalar(sc1[:, :], sc1[:, :], -1e30, None, op0=ALU.mult))
        V(nc.vector.tensor_copy(toQ4[:, :], sc1[:, :]))
        V(nc.vector.tensor_scalar(sc1[:, :], iota[:, :], pidx[:, 0:1], None,
                                   op0=ALU.is_gt))
        V(nc.vector.tensor_scalar(sc1[:, :], sc1[:, :], -1e30, None, op0=ALU.mult))
        V(nc.vector.tensor_copy(toQ0[:, :], sc1[:, :]))
        io2 = self.sbuf("FC_io2", [128, 1152], stack=fcs)
        self.mk(nc.gpsimd.iota(io2[:, :], pattern=[[1, 1152]], base=0,
                               channel_multiplier=0,
                               allow_small_or_imprecise_dtypes=True), "gp")
        self.w("ve", "gp")
        V(nc.vector.tensor_scalar(io2[:, :], io2[:, :], 512.0, None, op0=ALU.is_lt))
        V(nc.vector.tensor_scalar(io2[:, :], io2[:, :], -1e30, None, op0=ALU.mult))
        V(nc.vector.tensor_copy(H0[:, :], io2[:, :]))
        ld = self.dma("sy", hm[:, :], self.t["hmask"].ap()[:, :], sem="d0")
        self.w("ve", "d0", ld)
        nc.vector.tensor_scalar(hm1[:, :], hm[:, :], -1.0, None, op0=ALU.mult)
        self.mk(nc.vector.nop(), "ve"); self.flush("ve")
        nc.vector.tensor_scalar(hm1[:, :], hm1[:, :], 1.0, None, op0=ALU.add)
        self.mk(nc.vector.nop(), "ve"); self.flush("ve")
        self.mk(nc.vector.tensor_scalar(H0[:, :], H0[:, :], hm1[:, 0:1], None,
                                        op0=ALU.mult), "ve")
        self.flush("ve")
        fcs.close()

    def _hb_cast(self, hb, c0, width):
        nc = self.nc
        cur = self.t["xf"].ap()
        mrk = None
        for m in range(8):
            off = c0 if m < 4 else c0 - 1
            ins = nc.vector.tensor_copy(hb[:, m * 512:m * 512 + width],
                                        cur[:, m * TE + off:m * TE + off + width])
        mrk = self.mk(ins, "ve")
        self.flush("ve")
        return mrk

    def phase_ffn_qkv(self, l):
        nc = self.nc
        from contextlib import ExitStack
        self._ffn_stack = ExitStack()
        st = self._ffn_stack
        PS = self.PS.ap()
        qrS = self.sbuf("qrS", [65, FH * TOK], BF16, stack=st)
        krS = self.sbuf("krS", [65, FH * EXT], BF16, stack=st)
        vtok = self.sbuf("vtok", [128, (EXT // 128) * 512], BF16, stack=st)
        self._qk_stack = ExitStack()
        qst = self._qk_stack
        hb = self.sbuf("FQ_hb", [128, 8 * 512], BF16, stack=qst)
        wgt = self.sbuf("FQ_w", [128, 8 * 512], BF16, stack=qst)
        qb = self.sbuf("FQ_qb", [128, 512], BF16, stack=qst)
        sqq = self.sbuf("FQ_sq", [128, 512], BF16, stack=qst)
        RqT = self.sbuf("FQ_RqT", [128, 128], BF16, stack=qst)
        RkT = self.sbuf("FQ_RkT", [128, 128], BF16, stack=qst)
        qscol = self.sbuf("FQ_qs", [128, 1], stack=qst)
        kscol = self.sbuf("FQ_ks", [128, 1], stack=qst)
        csA = self.sbuf("FQ_csA", [FD, 512], BF16, stack=qst)
        snA = self.sbuf("FQ_snA", [FD, 512], BF16, stack=qst)
        rows = self.sbuf("FQ_rows", [1, 512], stack=qst)
        rows_bf = self.sbuf("FQ_rowsb", [1, 2 * 512], BF16, stack=qst)
        tA = sqq[0:64, :]
        tB = qb[0:64, :]
        fqkv = self.t["fqkv"].ap()
        epsc = self.t["epscol"]
        self.phase_barrier()
        ld = self.dma("sy", qscol[:, :], self.t["fqs"].ap()[:, l:l + 1], sem="d0")
        self.dma("sy", kscol[:, :], self.t["fks"].ap()[:, l:l + 1], sem="d0")
        ld = self.svals["d0"]
        self.w("ve", "d0", ld)
        nc.vector.tensor_scalar(RqT[:, :], self.t["Rsign"][:, :], qscol[:, 0:1],
                                None, op0=ALU.mult)
        self.mk(nc.vector.tensor_scalar(RkT[:, :], self.t["Rsign"][:, :],
                                        kscol[:, 0:1], None, op0=ALU.mult), "ve")
        self.flush("ve")
        nc.vector.memset(krS[64:65, :], 1.0)
        self.mk(nc.vector.nop(), "ve"); self.flush("ve")

        for m in range(8):
            self.dma("gp", wgt[:, m * 512:m * 512 + 512],
                     fqkv[l * DM + m * 128:l * DM + m * 128 + 128, 1024:1536],
                     sem="d2")
        wld = self.svals["d2"]
        self.w("te", "d2", wld)
        ve_pv = {}
        for i5 in range(EXT // 512):
            self.w("ve", "te", self.svals["te"])
            hv = self._hb_cast(hb, 1 + 512 * i5, 512)
            if i5 == 0 and "hbV" in self.dev:
                self.w("sy", "ve", hv)
                d = self.dram_out("dev_hbV", [128, 8 * 512], BF16)
                self.dma("sy", d.ap(), hb.ap()[:, :], sem="d3")
                self.w("sy", "d3")
                self.w("te", "d3", self.svals["d3"])
            self.w("te", "ve", hv)
            for u in range(4):
                ci = 4 * i5 + u
                bk = ci % 2
                mm = PS[:, bk * 512:bk * 512 + 512]
                if ci >= 2:
                    self.w("te", "ve", ve_pv[ci - 2])
                for m in range(8):
                    ins = nc.tensor.matmul(
                        mm, hb[:, m * 512 + 128 * u:m * 512 + 128 * u + 128],
                        wgt[:, m * 512:m * 512 + 512],
                        start=(m == 0), stop=(m == 7))
                te_m = self.mk(ins, "te")
                self.w("ve", "te", te_m)
                ve_pv[ci] = self.mk(nc.vector.tensor_copy(
                    vtok[:, ci * 512:ci * 512 + 512], mm), "ve")

        import os as _os
        ve_fin = {}
        it = 0
        for which in (() if _os.environ.get("FQSTOP") == "V" else ("q", "k")):
            co = 0 if which == "q" else 512
            ntc = TOK // 512 if which == "q" else EXT // 512
            base_c0 = HALO if which == "q" else 1
            dstS, seg = (qrS, TOK) if which == "q" else (krS, EXT)
            RT = RqT if which == "q" else RkT
            scol = qscol if which == "q" else kscol
            csn = ("fcsq", "fsnq") if which == "q" else ("fcsk", "fsnk")
            wid = TOK if which == "q" else EXT
            self.w("gp", "ve", self.svals["ve"])
            self.w("gp", "te", self.svals["te"])
            for m in range(8):
                self.dma("gp", wgt[:, m * 512:m * 512 + 512],
                         fqkv[l * DM + m * 128:l * DM + m * 128 + 128,
                              co:co + 512], sem="d2")
            wld = self.svals["d2"]
            self.w("te", "d2", wld)
            for tc in range(ntc):
                self.w("ve", "te", self.svals["te"])
                hv = self._hb_cast(hb, base_c0 + 512 * tc, 512)
                self.w("gp", "ve", self.svals["ve"])
                self.dma("gp", csA[:, :], self.t[csn[0]].ap()[:, 512 * tc:512 * tc + 512],
                         sem="d2")
                self.dma("gp", snA[:, :], self.t[csn[1]].ap()[:, 512 * tc:512 * tc + 512],
                         sem="d2")
                tld = self.svals["d2"]
                self.w("ve", "d2", tld)
                self.mk(nc.vector.tensor_scalar(csA[:, :], csA[:, :],
                                                scol[0:64, 0:1], None, op0=ALU.mult),
                        "ve")
                self.flush("ve")
                self.w("te", "ve", hv)
                for b in range(4):
                    bk = it % 2
                    mm = PS[:, bk * 512:bk * 512 + 512]
                    if it >= 2:
                        self.w("te", "ve", ve_fin[it - 2])
                    for m in range(8):
                        ins = nc.tensor.matmul(
                            mm, wgt[:, m * 512 + b * 128:m * 512 + b * 128 + 128],
                            hb[:, m * 512:m * 512 + 512],
                            start=(m == 0), stop=(m == 7))
                    te_m = self.mk(ins, "te")
                    if _os.environ.get("FQSTOP") == "MM":
                        ve_fin[it] = self.mk(nc.vector.nop(), "ve")
                        it += 1
                        continue
                    self.w("sc", "te", te_m)
                    if it >= 1:
                        self.w("sc", "ve", ve_fin[it - 1])
                    sc_e = self.mk(nc.scalar.copy(qb[:, :], mm), "sc")
                    self.flush("sc")
                    sc_q = self.mk(nc.scalar.activation(sqq[:, :], qb[:, :],
                                                        AF.Square), "sc")
                    self.w("te", "sc", sc_q)
                    if it >= 1:
                        self.w("te", "ve", ve_fin[it - 1])
                    nc.tensor.matmul(PS[0:64, 1024:1536], RT[0:64, 0:64],
                                     qb[0:64, :], start=True, stop=True)
                    nc.tensor.matmul(PS[0:64, 1536:2048], RT[64:128, 64:128],
                                     qb[64:128, :], start=True, stop=True)
                    nc.tensor.matmul(PS[0:64, 2048:2560],
                                     self.t["identbS"][64:128, 0:64],
                                     qb[64:128, :], start=True, stop=True)
                    nc.tensor.matmul(PS[0:1, 2560:3072], self.t["onescol"][0:64, 0:1],
                                     sqq[0:64, :], start=True, stop=True)
                    te_r = self.mk(nc.tensor.matmul(
                        PS[0:1, 3072:3584], self.t["onescol"][64:128, 0:1],
                        sqq[64:128, :], start=True, stop=True), "te")
                    if _os.environ.get("FQSTOP") == "EP1":
                        ve_fin[it] = self.mk(nc.vector.nop(), "ve")
                        it += 1
                        continue
                    self.w("ve", "te", te_r)
                    for par in range(2):
                        if par == 1:
                            self.w("ve", "sc", sc_rb)
                        gp_m = self.mk(nc.vector.tensor_scalar(
                            rows[0:1, 0:512], PS[0:1, 2560 + 512 * par:3072 + 512 * par],
                            1e-24, None, op0=ALU.max), "ve")
                        self.flush("ve")
                        self.w("sc", "ve", gp_m)
                        self.mk(nc.scalar.activation(rows[0:1, 0:512],
                                                     rows[0:1, 0:512], AF.Ln), "sc")
                        self.flush("sc")
                        self.mk(nc.scalar.activation(rows[0:1, 0:512],
                                                     rows[0:1, 0:512],
                                                     AF.Exp, scale=-0.5), "sc")
                        self.flush("sc")
                        sc_rb = self.mk(nc.scalar.copy(
                            rows_bf[0:1, 512 * par:512 * par + 512],
                            rows[0:1, 0:512]), "sc")
                        self.flush("sc")
                    self.w("te", "sc", sc_rb)
                    nc.tensor.matmul(PS[0:64, 2560:3072], self.t["onesb"][0:1, 0:64],
                                     rows_bf[0:1, 0:512], start=True, stop=True)
                    te_bc = self.mk(nc.tensor.matmul(
                        PS[0:64, 3072:3584], self.t["onesb"][0:1, 0:64],
                        rows_bf[0:1, 512:1024], start=True, stop=True), "te")
                    if _os.environ.get("FQSTOP") == "EP2":
                        ve_fin[it] = self.mk(nc.vector.nop(), "ve")
                        it += 1
                        continue
                    self.w("ve", "te", te_bc)
                    for par in range(2):
                        h = 2 * b + par
                        dcol = h * seg + 512 * tc
                        csl = csA[0:64, 0:512]
                        snl = snA[0:64, 0:512]
                        rsl = PS[0:64, 2560 + 512 * par:3072 + 512 * par]
                        rps = PS[0:64, 1024 + 512 * par:1536 + 512 * par]
                        if par == 0:
                            nc.vector.tensor_mul(tA, qb[0:64, :], csl)
                        else:
                            nc.vector.tensor_mul(tA, PS[0:64, 2048:2560], csl)
                        self.mk(nc.vector.nop(), "ve"); self.flush("ve")
                        nc.vector.tensor_mul(tB, rps, snl)
                        self.mk(nc.vector.nop(), "ve"); self.flush("ve")
                        if _os.environ.get("FQNOROT") != "1":
                            nc.vector.tensor_add(tA, tA, tB)
                            self.mk(nc.vector.nop(), "ve"); self.flush("ve")
                        ve_f = self.mk(nc.vector.tensor_mul(
                            dstS[0:64, dcol:dcol + 512], tA, rsl), "ve")
                        self.flush("ve")
                    ve_fin[it] = ve_f
                    it += 1
        self._qk_stack.close()
        self.phase_barrier()
        if "qrS" in self.dev:
            d = self.dram_out(f"dev_qrS{l}", [65, FH * TOK], BF16)
            self.dma("sy", d.ap(), qrS.ap()[:, :], sem="d3")
            d = self.dram_out(f"dev_krS{l}", [65, FH * EXT], BF16)
            self.dma("sy", d.ap(), krS.ap()[:, :], sem="d3")
            d = self.dram_out(f"dev_vtok{l}", [128, (EXT // 128) * 512], BF16)
            self.dma("sy", d.ap(), vtok.ap()[:, :], sem="d3")
            self.w("sy", "d3")


    def phase_ffn_attn(self, l):
        import os as _os
        def _os_env(k):
            return _os.environ.get(k) == "1"
        nc = self.nc
        from contextlib import ExitStack
        st = self._ffn_stack
        ast = ExitStack()
        self._at_stack = ast
        PS = self.PS.ap()
        PSb = self.PSb.ap()
        qrS = self.t["qrS"]
        krS = self.t["krS"]
        vtok = self.t["vtok"]
        o_fm = self.sbuf("o_fm", [128, 4 * TOK], BF16, stack=st)
        PT = self.sbuf("FA_PT", [128, 8 * 512], BF16, stack=ast)
        JE, SE, W0, SB = 0, 640, 896, 1536
        mxp = self.sbuf("FA_mxp", [128, 4], stack=ast)
        dcol = self.sbuf("FA_dcol", [128, 4], stack=ast)
        mnc = self.sbuf("FA_mnc", [128, 4], BF16, stack=ast)
        mn2c = self.sbuf("FA_mn2", [128, 4], BF16, stack=ast)
        mrow = self.sbuf("FA_mrow", [4, 128], BF16, stack=ast)
        H0 = self.t["H0"]
        toQ4, toQ0 = self.t["trioffQ4"], self.t["trioffQ0"]
        tri_le, tri_ge = self.t["tri_le"], self.t["tri_ge"]
        hmc, hm1 = self.t["hmcol"], self.t["hm1col"]
        idb = self.t["identb"]

        def G(ins):
            v = self.mk(ins, "gp")
            self.flush("gp")
            return v

        sb_ev = {}
        pt_ev = {}
        ve_o = None
        tr_prev = None
        dma_tr = None
        worder = [3, 2, 1, 0] if _os_env("FAREV") else list(range(4))
        for w in worder:
            for h in range(FH):
                hq0, hk0 = h * TOK, h * EXT
                qb0 = hq0 + 512 * w
                kb0 = hk0 + 512 * w
                for qc in range(4):
                    key = len(sb_ev)
                    reg = qc % 2
                    QM = PS[:, reg * 1024:reg * 1024 + 640]
                    if key >= 2:
                        self.w("te", "sc", sb_ev[key - 2])
                    nc.tensor.matmul(QM[:, 0:512],
                                     qrS[0:64, qb0 + 128 * qc:qb0 + 128 * qc + 128],
                                     krS[0:64, kb0 + 128 * qc:kb0 + 128 * qc + 512],
                                     start=True, stop=True)
                    te_qm = self.mk(nc.tensor.matmul(
                        QM[:, 512:640],
                        qrS[0:64, qb0 + 128 * qc:qb0 + 128 * qc + 128],
                        krS[0:64, kb0 + 128 * qc + 512:kb0 + 128 * qc + 640],
                        start=True, stop=True), "te")
                    self.w("sc", "te", te_qm)
                    self.w("sc", "gp", self.svals["gp"])
                    sb_ev[key] = self.mk(nc.scalar.copy(PT[:, SB:SB + 640], QM),
                                         "sc")
                    self.flush("sc")
                    self.w("gp", "sc", sb_ev[key])
                    if w == 0 and not _os_env("FANOW0"):
                        G(nc.gpsimd.tensor_tensor(
                            PT[:, W0:W0 + 640], H0[:, 128 * qc:128 * qc + 640],
                            PT[:, SB:SB + 640], op=ALU.add))
                        SO = W0
                    else:
                        SO = SB
                    G(nc.gpsimd.tensor_tensor(PT[:, SO:SO + 128],
                                              PT[:, SO:SO + 128],
                                              toQ4[:, :], op=ALU.add))
                    G(nc.gpsimd.tensor_tensor(PT[:, SO + 512:SO + 640],
                                              PT[:, SO + 512:SO + 640],
                                              toQ0[:, :], op=ALU.add))
                    self.w("ve", "gp", self.svals["gp"])
                    def V(ins):
                        v = self.mk(ins, "ve")
                        self.flush("ve")
                        return v
                    V(nc.vector.tensor_reduce(mxp[:, 3:4], PT[:, SO:SO + 640],
                                              axis=AX.X, op=ALU.max))
                    ve_mn = self.mk(nc.vector.tensor_scalar(mnc[:, qc:qc + 1],
                                                            mxp[:, 3:4], -1.0, None,
                                                            op0=ALU.mult), "ve")
                    self.w("sc", "ve", ve_mn)
                    bias = mnc[:, qc:qc + 1]
                    sc_x = self.mk(nc.scalar.activation(
                        PT[:, JE:JE + 640], PT[:, SO:SO + 640], AF.Exp,
                        bias=bias, accum_out=dcol[:, 3:4]), "sc")
                    self.flush("sc")
                    sc_ln = self.mk(nc.scalar.activation(dcol[:, 3:4], dcol[:, 3:4],
                                                         AF.Ln), "sc")
                    self.flush("sc")
                    self.w("ve", "sc", sc_ln)
                    V(nc.vector.tensor_add(mxp[:, 3:4], mxp[:, 3:4], dcol[:, 3:4]))
                    self.mk(nc.vector.tensor_scalar(mn2c[:, qc:qc + 1], mxp[:, 3:4],
                                                    -1.0, None, op0=ALU.mult), "ve")
                self.w("te", "ve", self.svals["ve"])
                if dma_tr is not None:
                    self.w("te", "d0", dma_tr)
                tr = self.mk(nc.tensor.transpose(PSb[0:4, 0:128], mn2c[:, 0:4],
                                                 idb[:, :]), "te")
                self.w("ve", "te", tr)
                ve_tr = self.mk(nc.vector.tensor_copy(mrow[:, :], PSb[0:4, 0:128]),
                                "ve")
                self.w("sy", "ve", ve_tr)
                dma_tr = self.dma("sy", qrS[64:65, qb0:qb0 + 512], mrow[:, :],
                                  sem="d0")
                self.w("te", "d0", dma_tr)
                tr_prev = tr
                ve_pt = None
                kb2 = len(pt_ev)
                for kc in range(8):
                    key2 = len(pt_ev)
                    qlo = 128 * max(0, kc - 4)
                    qhi = 128 * (min(3, kc) + 1)
                    wd = qhi - qlo
                    reg = kc % 2
                    ST = PS[:, 2048 + 512 * reg:2048 + 512 * reg + wd]
                    if key2 >= 2:
                        self.w("te", "sc", pt_ev[key2 - 2])
                    te_st = self.mk(nc.tensor.matmul(
                        ST, krS[0:65, kb0 + 128 * kc:kb0 + 128 * kc + 128],
                        qrS[0:65, qb0 + qlo:qb0 + qhi],
                        start=True, stop=True), "te")
                    self.w("sc", "te", te_st)
                    pt_ev[key2] = self.mk(nc.scalar.activation(
                        PT[:, 512 * kc:512 * kc + wd], ST, AF.Exp), "sc")
                    self.flush("sc")
                    self.w("ve", "sc", pt_ev[key2])
                    if w == 0 and kc <= 3:
                        ve_pt = self.mk(nc.vector.tensor_scalar(
                            PT[:, 512 * kc:512 * kc + wd],
                            PT[:, 512 * kc:512 * kc + wd],
                            hmc[:, 0:1], None, op0=ALU.mult), "ve")
                        self.flush("ve")
                    if kc <= 3:
                        off = 512 * kc + (128 * kc - qlo)
                        ve_pt = self.mk(nc.vector.tensor_mul(
                            PT[:, off:off + 128], PT[:, off:off + 128],
                            tri_ge[:, :]), "ve")
                    if kc >= 4:
                        off = 512 * kc
                        ve_pt = self.mk(nc.vector.tensor_mul(
                            PT[:, off:off + 128], PT[:, off:off + 128],
                            tri_le[:, :]), "ve")
                self.w("te", "sc", pt_ev[kb2 + 7])
                if ve_pt is not None:
                    self.w("te", "ve", ve_pt)
                if ve_o is not None:
                    self.w("te", "ve", ve_o)
                p0 = 64 * (h % 2)
                for qc in range(4):
                    for i, kc in enumerate(range(qc, qc + 5)):
                        ci = 4 * w + kc
                        off = 512 * kc + (128 * qc - 128 * max(0, kc - 4))
                        ins = nc.tensor.matmul(
                            PS[p0:p0 + 64, 3072 + 128 * qc:3072 + 128 * qc + 128],
                            vtok[:, 512 * ci + 64 * h:512 * ci + 64 * h + 64],
                            PT[:, off:off + 128],
                            start=(i == 0), stop=(i == 4))
                te_av = self.mk(ins, "te")
                self.w("ve", "te", te_av)
                ve_o = self.mk(nc.vector.tensor_copy(
                    o_fm[p0:p0 + 64, (h // 2) * TOK + 512 * w:
                         (h // 2) * TOK + 512 * w + 512],
                    PS[p0:p0 + 64, 3072:3584]), "ve")
        ast.close()
        if "ofm" in self.dev:
            self.phase_barrier()
            d = self.dram_out(f"dev_ofm{l}", [128, 4 * TOK], BF16)
            self.dma("sy", d.ap(), o_fm.ap()[:, :], sem="d3")
            self.w("sy", "d3")

    def phase_f0att(self):
        self.phase_ffn_attn(0)


    def phase_ffn_mid(self, l):
        nc = self.nc
        from contextlib import ExitStack
        PS = self.PS.ap()
        cur = self.t["xf"]
        o_fm = self.t["o_fm"]
        fwo = self.t["fwo"].ap()

        def V(ins):
            v = self.mk(ins, "ve")
            self.flush("ve")
            return v

        mst0 = ExitStack()
        wob = self.sbuf("FM_wo", [128, 32 * 128], BF16, stack=mst0)
        self.w("gp", "ve", self.svals["ve"])
        self.w("gp", "te", self.svals["te"])
        for m in range(8):
            for k in range(4):
                self.dma("gp", wob[:, (m * 4 + k) * 128:(m * 4 + k) * 128 + 128],
                         fwo[l * 512 + k * 128:l * 512 + k * 128 + 128,
                             m * 128:m * 128 + 128], sem="d2")
        self.w("te", "d2", self.svals["d2"])
        ve_w = {}
        for m in range(8):
            for tc in range(4):
                it = m * 4 + tc
                bk = it % 2
                mm = PS[:, bk * 512:bk * 512 + 512]
                if it >= 2:
                    self.w("te", "ve", ve_w[it - 2])
                for k in range(4):
                    ins = nc.tensor.matmul(
                        mm, wob[:, (m * 4 + k) * 128:(m * 4 + k) * 128 + 128],
                        o_fm[:, k * TOK + 512 * tc:k * TOK + 512 * tc + 512],
                        start=(k == 0), stop=(k == 3))
                te_m = self.mk(ins, "te")
                self.w("ve", "te", te_m)
                dst = cur[:, m * TE + HALO + 512 * tc:m * TE + HALO + 512 * tc + 512]
                ve_w[it] = V(nc.vector.tensor_add(dst, dst, mm))
        mst0.close()
        self._ffn_stack.close()

        mst = ExitStack()
        self._mid_stack = mst
        ain = self.t["ag_in"].ap()
        aout = self.t["ag_out"]
        self.w("gp", "ve", self.svals["ve"])
        for m in range(8):
            self.dma("gp", ain[m * 128:(m + 1) * 128, :],
                     cur[:, m * TE + TE - 514:m * TE + TE], sem="d2")
        self.w("gp", "d2", self.svals["d2"])
        self.w("gp", "ix", self.svals["ix"])
        cc = self.mk(nc.gpsimd.collective_compute(
            "AllGather", ALU.bypass, replica_groups=[list(range(NCORE))],
            ins=[self.t["ag_in"].ap().opt()], outs=[self.t["ag_out"].ap().opt()]),
            "cc")
        xm1c = self.sbuf("FM_xm1", [128, 8], stack=mst)
        ex2 = ExitStack()
        pvx = self.sbuf("FM_pvx", [128, 8], U32, stack=ex2)
        gbuf = self.sbuf("FM_gb", [128, 2 * 514], BF16, stack=ex2)
        ld = self.dma("sy", pvx[:, :], self.t["pvidx"].ap()[:, :], sem="d1")
        self.w("gp", "d1", ld)
        self.w("gp", "cc", cc)
        hmc = self.t["hmcol"]
        ixm = {}
        for m in range(9):
            mm_ = min(m, 7)
            gslc = gbuf[:, (m % 2) * 514:(m % 2) * 514 + 514]
            if m >= 2:
                self.w("gp", "ve", vg_m[m - 2])
            ins = nc.gpsimd.indirect_dma_start(
                out=gslc, out_offset=None, in_=aout.ap()[:, :],
                in_offset=IndirectOffsetOnAxis(ap=pvx[:, mm_:mm_ + 1], axis=0),
                element_offset=0)
            ixm[m] = self.mk(ins, "ix", 16)
            if m == 0:
                vg_m = {}
                continue
            mc = m - 1
            gprev = gbuf[:, (mc % 2) * 514:(mc % 2) * 514 + 514]
            self.w("ve", "ix", ixm[mc] + 16)
            V(nc.vector.tensor_scalar(cur[:, mc * TE:mc * TE + 513],
                                      gprev[:, 1:514], hmc[:, 0:1], None,
                                      op0=ALU.mult))
            vg_m[mc] = V(nc.vector.tensor_scalar(xm1c[:, mc:mc + 1], gprev[:, 0:1],
                                                 hmc[:, 0:1], None, op0=ALU.mult))

        ex2.close()
        zb = self.sbuf("FM_zb", [128, 8 * TE], BF16, stack=mst)
        h2 = self.sbuf("FM_h2", [128, 11 * TE], BF16, stack=mst)
        hz = h2[:, 0:8 * 512]
        sqz = h2[:, 8 * 512:8 * 512 + 512]
        zt = h2[:, 8 * 512 + 512:8 * 512 + 1024]
        glb = self.sbuf("FM_gl", [128, 512], BF16, stack=mst)
        w1t = self.sbuf("FM_w1", [128, 2 * 16 * 128], BF16, stack=mst)
        w2t = self.sbuf("FM_w2", [128, 11 * 128], BF16, stack=mst)
        rowm = self.sbuf("FM_rows", [1, 3 * 512], stack=mst)
        rows_bf = self.sbuf("FM_rowsb", [1, 2 * 512], BF16, stack=mst)
        onesr = self.sbuf("FM_ones", [1, 512], BF16, stack=mst)
        b1c = self.sbuf("FM_b1c", [128, 2 * NHB], stack=mst)
        b2c = self.sbuf("FM_b2c", [128, 8], stack=mst)
        lgc = self.sbuf("FM_lgc", [128, 8], stack=mst)
        lgrb = self.sbuf("FM_lgr", [1, DM], BF16, stack=mst)
        lbrb = self.sbuf("FM_lbr", [1, DM], BF16, stack=mst)
        nc.vector.memset(onesr[0:1, :], 1.0)
        self.mk(nc.vector.nop(), "ve"); self.flush("ve")
        self.dma("sy", b1c[:, :], self.t["fb1c"].ap()[l * 128:(l + 1) * 128, :],
                 sem="d1")
        self.dma("sy", b2c[:, :], self.t["fb2c"].ap()[l * 128:(l + 1) * 128, :],
                 sem="d1")
        self.dma("sy", lgc[:, :], self.t["flgc"].ap()[l * 128:(l + 1) * 128, :],
                 sem="d1")
        self.dma("gp", lgrb[0:1, :], self.t["flgr"].ap()[l:l + 1, :], sem="d2")
        self.dma("gp", lbrb[0:1, :], self.t["flbr"].ap()[l:l + 1, :], sem="d2")
        self.w("ve", "d1", self.svals["d1"])
        self.w("ve", "d2", self.svals["d2"])
        self.w("te", "d2", self.svals["d2"])
        epsc = self.t["epscol"]
        onescol = self.t["onescol"]
        if l == 0:
            tchunks = [(i * 512, 512) for i in range(5)] + [(2560, 1)]
            cbase = 0
        else:
            tchunks = [(HALO + i * 512, 512) for i in range(4)]
            cbase = HALO
        fw1 = self.t["fw1"].ap()
        fw2 = self.t["fw2"].ap()
        S1B, S2B = 2560, 3072
        BRB, BNB = 1024, 1536
        W2B = 2048
        for ti, (c0, wd) in enumerate(reversed(tchunks)):
            self.w("ve", "te", self.svals["te"])
            for m in range(8):
                off = c0 if m < 4 else c0 - 1
                if off < 0:
                    ins = nc.vector.tensor_copy(
                        hz[:, m * 512 + 1:m * 512 + wd],
                        cur[:, m * TE:m * TE + wd - 1])
                    nc.vector.tensor_copy(hz[:, m * 512:m * 512 + 1],
                                          xm1c[:, m:m + 1])
                else:
                    ins = nc.vector.tensor_copy(
                        hz[:, m * 512:m * 512 + wd],
                        cur[:, m * TE + off:m * TE + off + wd])
            hv = self.mk(ins, "ve")
            self.flush("ve")
            self.w("te", "ve", hv)
            self.w("sc", "ve", hv)
            te_q = None
            for m in range(8):
                if te_q is not None:
                    self.w("sc", "te", te_q)
                sc_q = self.mk(nc.scalar.activation(sqz[:, 0:wd],
                                                    hz[:, m * 512:m * 512 + wd],
                                                    AF.Square), "sc")
                self.flush("sc")
                self.w("te", "sc", sc_q)
                nc.tensor.matmul(PS[0:1, S1B:S1B + wd], onescol[:, 0:1],
                                 hz[:, m * 512:m * 512 + wd],
                                 start=(m == 0), stop=(m == 7))
                te_q = self.mk(nc.tensor.matmul(
                    PS[0:1, S2B:S2B + wd], onescol[:, 0:1], sqz[:, 0:wd],
                    start=(m == 0), stop=(m == 7)), "te")
            self.w("ve", "te", te_q)
            V(nc.vector.tensor_scalar(rowm[0:1, 0:wd], PS[0:1, S1B:S1B + wd],
                                      1.0 / DM, None, op0=ALU.mult))
            V(nc.vector.tensor_scalar(rowm[0:1, 512:512 + wd], PS[0:1, S2B:S2B + wd],
                                      1.0 / DM, None, op0=ALU.mult))
            V(nc.vector.tensor_mul(rowm[0:1, 1024:1024 + wd], rowm[0:1, 0:wd],
                                   rowm[0:1, 0:wd]))
            vv = V(nc.vector.tensor_sub(rowm[0:1, 512:512 + wd],
                                        rowm[0:1, 512:512 + wd],
                                        rowm[0:1, 1024:1024 + wd]))
            self.w("sc", "ve", vv)
            self.mk(nc.scalar.activation(rowm[0:1, 512:512 + wd],
                                         rowm[0:1, 512:512 + wd], AF.Ln,
                                         bias=epsc[0:1, 0:1]), "sc")
            self.flush("sc")
            self.mk(nc.scalar.activation(rowm[0:1, 512:512 + wd],
                                         rowm[0:1, 512:512 + wd], AF.Exp,
                                         scale=-0.5), "sc")
            self.flush("sc")
            sc_rb = self.mk(nc.scalar.copy(rows_bf[0:1, 0:wd],
                                           rowm[0:1, 512:512 + wd]), "sc")
            self.flush("sc")
            self.w("ve", "sc", sc_rb)
            V(nc.vector.tensor_mul(rowm[0:1, 1024:1024 + wd], rowm[0:1, 0:wd],
                                   rowm[0:1, 512:512 + wd]))
            vn = V(nc.vector.tensor_scalar(rowm[0:1, 1024:1024 + wd],
                                           rowm[0:1, 1024:1024 + wd], -1.0,
                                           None, op0=ALU.mult))
            self.w("sc", "ve", vn)
            sc_nb = self.mk(nc.scalar.copy(rows_bf[0:1, 512:512 + wd],
                                           rowm[0:1, 1024:1024 + wd]), "sc")
            self.flush("sc")
            self.w("te", "sc", sc_nb)
            te_br = self.mk(nc.tensor.matmul(
                PS[:, BRB:BRB + wd], self.t["onesb"][0:1, 0:128],
                rows_bf[0:1, 0:wd], start=True, stop=True), "te")
            for m in range(8):
                nc.tensor.matmul(PS[:, BNB:BNB + wd],
                                 lgrb[0:1, m * 128:m * 128 + 128],
                                 rows_bf[0:1, 512:512 + wd],
                                 start=True, stop=False)
                te_bn = self.mk(nc.tensor.matmul(
                    PS[:, BNB:BNB + wd], lbrb[0:1, m * 128:m * 128 + 128],
                    onesr[0:1, 0:wd], start=False, stop=True), "te")
                self.w("ve", "te", te_bn)
                nc.vector.tensor_mul(zt[:, 0:wd], hz[:, m * 512:m * 512 + wd],
                                     PS[:, BRB:BRB + wd])
                self.mk(nc.vector.nop(), "ve"); self.flush("ve")
                vz = V(nc.vector.scalar_tensor_tensor(
                    zb[:, m * TE + c0:m * TE + c0 + wd], zt[:, 0:wd],
                    lgc[:, m:m + 1], PS[:, BNB:BNB + wd],
                    op0=ALU.mult, op1=ALU.add))
                self.w("te", "ve", vz)
        for half in range(2):
            te_tl = {}
            sc_gl = None
            ve_h2 = None
            self.w("gp", "te", self.svals["te"])
            for jj in range(11):
                j = 11 * half + jj
                buf = jj % 2
                if jj >= 2:
                    self.w("gp", "te", te_tl[jj - 2])
                for k in range(8):
                    self.dma("gp", w1t[:, (buf * 16 + k) * 128:
                                       (buf * 16 + k) * 128 + 128],
                             fw1[l * DM + k * 128:l * DM + k * 128 + 128,
                                 256 * j:256 * j + 128], sem="d2")
                    self.dma("gp", w1t[:, (buf * 16 + 8 + k) * 128:
                                       (buf * 16 + 8 + k) * 128 + 128],
                             fw1[l * DM + k * 128:l * DM + k * 128 + 128,
                                 256 * j + 128:256 * j + 256], sem="d2")
                self.w("te", "d2", self.svals["d2"])
                for ti, (c0, wd) in enumerate(tchunks):
                    if sc_gl is not None:
                        self.w("te", "sc", sc_gl)
                    if ve_h2 is not None:
                        self.w("te", "ve", ve_h2)
                        self.w("sc", "ve", ve_h2)
                    for k in range(8):
                        nc.tensor.matmul(PS[:, 0:wd],
                                         w1t[:, (buf * 16 + k) * 128:
                                             (buf * 16 + k) * 128 + 128],
                                         zb[:, k * TE + c0:k * TE + c0 + wd],
                                         start=(k == 0), stop=(k == 7))
                        te_g = self.mk(nc.tensor.matmul(
                            PS[:, 512:512 + wd],
                            w1t[:, (buf * 16 + 8 + k) * 128:
                                (buf * 16 + 8 + k) * 128 + 128],
                            zb[:, k * TE + c0:k * TE + c0 + wd],
                            start=(k == 0), stop=(k == 7)), "te")
                    te_tl[jj] = te_g
                    self.w("sc", "te", te_g)
                    sc_gl = self.mk(nc.scalar.activation(
                        glb[:, 0:wd], PS[:, 512:512 + wd], AF.Gelu,
                        bias=b1c[:, 2 * j + 1:2 * j + 2]), "sc")
                    self.flush("sc")
                    self.w("ve", "sc", sc_gl)
                    ve_h2 = V(nc.vector.scalar_tensor_tensor(
                        h2[:, jj * TE + c0:jj * TE + c0 + wd], PS[:, 0:wd],
                        b1c[:, 2 * j:2 * j + 1], glb[:, 0:wd],
                        op0=ALU.add, op1=ALU.mult))
            self.w("te", "ve", ve_h2)
            ve_c = None
            te_w2 = None
            self.w("gp", "te", self.svals["te"])
            for m in range(8):
                buf = 0
                if m >= 1:
                    self.w("gp", "te", te_w2m[m - 1])
                for jj in range(11):
                    self.dma("gp", w2t[:, (buf * 11 + jj) * 128:
                                       (buf * 11 + jj) * 128 + 128],
                             fw2[l * HIDP + (11 * half + jj) * 128:
                                 l * HIDP + (11 * half + jj) * 128 + 128,
                                 m * 128:m * 128 + 128], sem="d2")
                self.w("te", "d2", self.svals["d2"])
                for ti, (c0, wd) in enumerate(tchunks):
                    if ve_c is not None:
                        self.w("te", "ve", ve_c)
                    for jj in range(11):
                        ins = nc.tensor.matmul(
                            PS[:, W2B:W2B + wd],
                            w2t[:, (buf * 11 + jj) * 128:(buf * 11 + jj) * 128 + 128],
                            h2[:, jj * TE + c0:jj * TE + c0 + wd],
                            start=(jj == 0), stop=(jj == 10))
                    te_w2 = self.mk(ins, "te")
                    try:
                        te_w2m[m] = te_w2
                    except NameError:
                        te_w2m = {m: te_w2}
                    self.w("ve", "te", te_w2)
                    dst = cur[:, m * TE + c0:m * TE + c0 + wd]
                    if half == 0:
                        ve_c = V(nc.vector.tensor_add(dst, dst, PS[:, W2B:W2B + wd]))
                    else:
                        ve_c = V(nc.vector.scalar_tensor_tensor(
                            dst, PS[:, W2B:W2B + wd], b2c[:, m:m + 1], dst,
                            op0=ALU.add, op1=ALU.add))
        mst.close()
        if "xl" in self.dev:
            self.phase_barrier()
            d = self.dram_out(f"dev_xl{l}", [128, 8 * TE])
            self.dma("sy", d.ap(), cur[:, :], sem="d3")
            self.w("sy", "d3")

    def phase_f0mid(self):
        self.phase_ffn_mid(0)


    def phase_final(self):
        nc = self.nc
        from contextlib import ExitStack
        fst = ExitStack()
        PS = self.PS.ap()
        cur = self.t["xf"]
        x1de = self.t["x1de"].ap()
        out = self.t["out"].ap()
        sqz = self.sbuf("FF_sq", [128, 512], BF16, stack=fst)
        x1b = self.sbuf("FF_x1", [128, 2 * 512], stack=fst)
        ob = self.sbuf("FF_ob", [128, 2 * 512], stack=fst)
        rowm = self.sbuf("FF_rows", [1, 3 * 512], stack=fst)
        onescol = self.t["onescol"]
        onescolf = self.t["onescolf"]
        onesf = self.t["onesf"]
        epsc = self.t["epscol"]

        def V(ins):
            v = self.mk(ins, "ve")
            self.flush("ve")
            return v

        self.phase_barrier()
        S1B, S2B, BRB, BNB = 2560, 3072, 1024, 1536
        ve_o = {}
        for tc in range(4):
            c0 = HALO + 512 * tc
            te_q = None
            for m in range(8):
                if te_q is not None:
                    self.w("sc", "te", te_q)
                sc_q = self.mk(nc.scalar.activation(
                    sqz[:, :], cur[:, m * TE + c0:m * TE + c0 + 512],
                    AF.Square), "sc")
                self.flush("sc")
                self.w("te", "sc", sc_q)
                if tc > 0 and m == 0:
                    self.w("te", "ve", self.svals["ve"])
                nc.tensor.matmul(PS[0:1, S1B:S1B + 512], onescolf[:, 0:1],
                                 cur[:, m * TE + c0:m * TE + c0 + 512],
                                 start=(m == 0), stop=(m == 7))
                te_q = self.mk(nc.tensor.matmul(
                    PS[0:1, S2B:S2B + 512], onescol[:, 0:1], sqz[:, :],
                    start=(m == 0), stop=(m == 7)), "te")
            self.w("ve", "te", te_q)
            V(nc.vector.tensor_scalar(rowm[0:1, 0:512], PS[0:1, S1B:S1B + 512],
                                      1.0 / DM, None, op0=ALU.mult))
            V(nc.vector.tensor_scalar(rowm[0:1, 512:1024], PS[0:1, S2B:S2B + 512],
                                      1.0 / DM, None, op0=ALU.mult))
            V(nc.vector.tensor_mul(rowm[0:1, 1024:1536], rowm[0:1, 0:512],
                                   rowm[0:1, 0:512]))
            vv = V(nc.vector.tensor_sub(rowm[0:1, 512:1024], rowm[0:1, 512:1024],
                                        rowm[0:1, 1024:1536]))
            self.w("sc", "ve", vv)
            self.mk(nc.scalar.activation(rowm[0:1, 512:1024], rowm[0:1, 512:1024],
                                         AF.Ln, bias=epsc[0:1, 0:1]), "sc")
            self.flush("sc")
            sc_r = self.mk(nc.scalar.activation(rowm[0:1, 512:1024],
                                                rowm[0:1, 512:1024],
                                                AF.Exp, scale=-0.5), "sc")
            self.flush("sc")
            self.w("ve", "sc", sc_r)
            V(nc.vector.tensor_mul(rowm[0:1, 1024:1536], rowm[0:1, 0:512],
                                   rowm[0:1, 512:1024]))
            vn = V(nc.vector.tensor_scalar(rowm[0:1, 1024:1536],
                                           rowm[0:1, 1024:1536], -1.0, None,
                                           op0=ALU.mult))
            self.w("te", "ve", vn)
            nc.tensor.matmul(PS[:, BRB:BRB + 512], onesf[0:1, :],
                             rowm[0:1, 512:1024], start=True, stop=True)
            te_b = self.mk(nc.tensor.matmul(
                PS[:, BNB:BNB + 512], onesf[0:1, :], rowm[0:1, 1024:1536],
                start=True, stop=True), "te")
            for m in range(8):
                b2 = m % 2
                key = tc * 8 + m
                if key >= 2:
                    self.w("sy", "ve", ve_o[key - 2])
                    self.w("sy", "d0", d0v[key - 2])
                ld = self.dma("sy", x1b[:, b2 * 512:b2 * 512 + 512],
                              x1de[m * 128:(m + 1) * 128, c0:c0 + 512], sem="d1")
                self.w("ve", "d1", ld)
                self.w("ve", "te", te_b)
                nc.vector.tensor_mul(ob[:, b2 * 512:b2 * 512 + 512],
                                     cur[:, m * TE + c0:m * TE + c0 + 512],
                                     PS[:, BRB:BRB + 512])
                self.mk(nc.vector.nop(), "ve"); self.flush("ve")
                nc.vector.tensor_add(ob[:, b2 * 512:b2 * 512 + 512],
                                     ob[:, b2 * 512:b2 * 512 + 512],
                                     PS[:, BNB:BNB + 512])
                self.mk(nc.vector.nop(), "ve"); self.flush("ve")
                ve_o[key] = V(nc.vector.tensor_add(ob[:, b2 * 512:b2 * 512 + 512],
                                                   ob[:, b2 * 512:b2 * 512 + 512],
                                                   x1b[:, b2 * 512:b2 * 512 + 512]))
                self.w("sy", "ve", ve_o[key])
                self.dma("sy", out[m * 128:(m + 1) * 128, 512 * tc:512 * tc + 512],
                         ob[:, b2 * 512:b2 * 512 + 512], sem="d0")
                try:
                    d0v[key] = self.svals["d0"]
                except NameError:
                    d0v = {key: self.svals["d0"]}
        fst.close()
        self.phase_barrier()

    def phase_f1qkv(self):
        self.phase_ffn_qkv(1)

    def phase_f1att(self):
        self.phase_ffn_attn(1)

    def phase_f1mid(self):
        self.phase_ffn_mid(1)

    def phase_f0qkv(self):
        self.phase_ffn_qkv(0)


def build_program(dev_dumps=(), upto="all"):
    from contextlib import ExitStack
    nc = bass.Bass("TRN2", target_bir_lowering=False)
    stack = ExitStack()
    kb = KB(nc, stack, dev_dumps)
    kb.declare()
    phases = ["consts", "stats", "qkv", "buckets", "pos", "scatter", "chunk",
              "combine", "exchange", "x1", "fconsts", "f0qkv", "f0att", "f0mid",
              "f1qkv", "f1att", "f1mid", "final"]
    if upto != "all":
        phases = phases[:phases.index(upto) + 1]
    with nc.Block() as block:
        @block.gpsimd
        def _(gp):
            with nc.allow_non_contiguous_dma("width-1 tail tiles"):
                for p in phases:
                    if hasattr(kb, "phase_" + p):
                        getattr(kb, "phase_" + p)()
            for attr in ("_ffn_stack", "lsh_stack", "ex_stack"):
                if hasattr(kb, attr):
                    try:
                        getattr(kb, attr).close()
                    except Exception:
                        pass
    stack.close()
    return nc, kb


def _ffn_host(inputs):
    d = {}
    wqkv = np.asarray(inputs["L0_ffn_wqkv"], np.float32)
    d["fqkv"] = np.ascontiguousarray(wqkv.reshape(2 * DM, 1536))
    wo = np.asarray(inputs["L0_ffn_wo"], np.float32)
    d["fwo"] = np.ascontiguousarray(wo.reshape(2 * 512, DM))
    w1 = np.asarray(inputs["L0_ffn_w1"], np.float32)
    b1 = np.asarray(inputs["L0_ffn_b1"], np.float32)
    w1P = np.zeros((2, DM, 2 * HIDP), np.float32)
    b1P = np.zeros((2, 2 * HIDP), np.float32)
    for j in range(NHB):
        lo, hi = 128 * j, min(128 * (j + 1), HID)
        w = hi - lo
        w1P[:, :, 256 * j:256 * j + w] = w1[:, :, lo:hi]
        w1P[:, :, 256 * j + 128:256 * j + 128 + w] = w1[:, :, HID + lo:HID + hi]
        b1P[:, 256 * j:256 * j + w] = b1[:, lo:hi]
        b1P[:, 256 * j + 128:256 * j + 128 + w] = b1[:, HID + lo:HID + hi]
    d["fw1"] = np.ascontiguousarray(w1P.reshape(2 * DM, 2 * HIDP))
    d["fb1c"] = np.ascontiguousarray(
        b1P.reshape(2, 2 * NHB, 128).transpose(0, 2, 1).reshape(2 * 128, 2 * NHB))
    w2 = np.asarray(inputs["L0_ffn_w2"], np.float32)
    w2P = np.zeros((2, HIDP, DM), np.float32)
    w2P[:, :HID] = w2
    d["fw2"] = np.ascontiguousarray(w2P.reshape(2 * HIDP, DM))
    b2 = np.asarray(inputs["L0_ffn_b2"], np.float32)
    d["fb2c"] = np.ascontiguousarray(
        b2.reshape(2, 8, 128).transpose(0, 2, 1).reshape(2 * 128, 8))
    lg = np.asarray(inputs["L0_ffn_ln_g"], np.float32)
    lb = np.asarray(inputs["L0_ffn_ln_b"], np.float32)
    d["flgc"] = np.ascontiguousarray(lg.reshape(2, 8, 128).transpose(0, 2, 1).reshape(2 * 128, 8))
    d["flbc"] = np.ascontiguousarray(lb.reshape(2, 8, 128).transpose(0, 2, 1).reshape(2 * 128, 8))
    d["flgr"] = np.ascontiguousarray(lg)
    d["flbr"] = np.ascontiguousarray(lb)
    qs = np.asarray(inputs["L0_ffn_q_scale"], np.float32)
    ks = np.asarray(inputs["L0_ffn_k_scale"], np.float32)
    d["fqs"] = np.ascontiguousarray(np.concatenate([qs, qs], axis=1).T)
    d["fks"] = np.ascontiguousarray(np.concatenate([ks, ks], axis=1).T)
    return d


def make_inmaps(inputs):
    C = host_constants()
    F = _ffn_host(inputs)
    csqp = np.concatenate([np.zeros((FD, 512), np.float32), C["csq"]], axis=1)
    snqp = np.concatenate([np.zeros((FD, 512), np.float32), C["snq"]], axis=1)
    cskp = np.concatenate([np.zeros((FD, 512), np.float32), C["csk"]], axis=1)
    snkp = np.concatenate([np.zeros((FD, 512), np.float32), C["snk"]], axis=1)
    x = np.ascontiguousarray(np.asarray(inputs["L0_attn_x"], np.float32).reshape(S, DM))
    xT = np.ascontiguousarray(x.T)
    wqkT = np.ascontiguousarray(np.asarray(inputs["L0_attn_w_qk"], np.float32).T)
    wvT = np.ascontiguousarray(np.asarray(inputs["L0_attn_w_v"], np.float32).T)
    woutT = np.ascontiguousarray(np.asarray(inputs["L0_attn_w_out"], np.float32).T)
    xTp = np.concatenate([np.zeros((DM, HALO), np.float32), xT], axis=1)
    maps = []
    for c in range(NCORE):
        maps.append(dict(
            xsl=np.ascontiguousarray(x[c * TOK:(c + 1) * TOK]),
            xT=xT,
            xTe=np.ascontiguousarray(xTp[:, c * TOK:c * TOK + TOK + HALO]),
            wqkT=np.ascontiguousarray(wqkT[:, 128 * c:128 * c + 128]),
            wvT=np.ascontiguousarray(wvT[:, 128 * c:128 * c + 128]),
            woutT=woutT,
            rot2=np.ascontiguousarray(C["rot2"][c]),
            fcsq=np.ascontiguousarray(C["csq"][:, TOK * c:TOK * c + TOK]),
            fsnq=np.ascontiguousarray(C["snq"][:, TOK * c:TOK * c + TOK]),
            fcsk=np.ascontiguousarray(cskp[:, TOK * c:TOK * c + EXT]),
            fsnk=np.ascontiguousarray(snkp[:, TOK * c:TOK * c + EXT]),
            hmask=np.full((128, 1), 0.0 if c == 0 else 1.0, np.float32),
            pvidx=(((c - 1) % NCORE) * DM
                   + np.arange(128, dtype=np.uint32)[:, None]
                   + 128 * np.arange(8, dtype=np.uint32)[None, :]).astype(np.uint32),
            **F,
        ))
    return maps


def kernel(**inputs):
    from concourse.bass_utils import run_bass_kernel_spmd
    maps = make_inmaps(inputs)
    nc, kb = build_program()
    res = run_bass_kernel_spmd(nc, maps, core_ids=list(range(NCORE)))
    out = np.zeros((S, DM), np.float32)
    for c in range(NCORE):
        out[c * TOK:(c + 1) * TOK] = np.asarray(res.results[c]["out"],
                                                dtype=np.float32).T
    return out.reshape(1, S, DM)
